# version 10
# baseline (speedup 1.0000x reference)
import numpy as np
import ml_dtypes
import concourse.bass as bass
import concourse.mybir as mybir
from concourse.bass_utils import run_bass_kernel_spmd

F32 = mybir.dt.float32
BF16 = mybir.dt.bfloat16
F8 = mybir.dt.float8e4
AF = mybir.ActivationFunctionType
ALU = mybir.AluOpType
BF = ml_dtypes.bfloat16

D = 2048
DI = 4096
DXB = 1024
DST = 128
HD = 64
NH = 64
NPAIR = 32
DCONV = 4
CONVD = DI + 2 * DXB
DIP = DI + CONVD + NH
IFF = 6144
CK = 128
EPS = 1e-5

ENGINES = ("sync", "vector", "scalar", "tensor", "gpsimd")


class Sem:
    __slots__ = ("name", "n", "handle")

    def __init__(self, name):
        self.name = name
        self.n = 0
        self.handle = None


class Op:
    __slots__ = ("fname", "args", "kw", "waits", "incs")

    def __init__(self, fname, args, kw):
        self.fname = fname
        self.args = args
        self.kw = kw
        self.waits = []
        self.incs = []


class Buf:
    __slots__ = ("ready", "frees", "sem")
    _n = 0

    def __init__(self, prog=None):
        self.ready = None
        self.frees = []
        if prog is not None:
            Buf._n += 1
            self.sem = prog.sem(f"b{Buf._n}")
        else:
            self.sem = None


def _ap_ranges(a):
    try:
        t = a.tensor
        esz = mybir.dt.size(a.dtype)
        off = a.offset * esz
        span = esz
        for st, ct in list(a.ap)[1:]:
            span += abs(st) * (ct - 1) * esz
        p0 = t.base_partition + (list(a.ap)[0][0] and 0)
        pc = list(a.ap)[0][1]
        return (t.name, off, off + span, p0, p0 + pc)
    except Exception:
        return None


def _overlap(r1, r2):
    if r1 is None or r2 is None:
        return True
    if r1[0] != r2[0]:
        return False
    return r1[1] < r2[2] and r2[1] < r1[2]


_OUT_KW = ("out", "dest")


class Prog:
    DRAIN_ENGS = ("vector", "scalar")

    def __init__(self):
        self.q = {e: [] for e in ENGINES}
        self.sems = {}
        self._written = {e: [] for e in ENGINES}

    def sem(self, name):
        if name not in self.sems:
            self.sems[name] = Sem(name)
        return self.sems[name]

    @staticmethod
    def _split_io(fname, args, kw):
        outs, ins = [], []
        aps = [a for a in args if hasattr(a, "tensor")]
        for k, v in kw.items():
            if hasattr(v, "tensor"):
                (outs if k in _OUT_KW else ins).append(v)
        if aps:
            outs.append(aps[0])
            ins.extend(aps[1:])
        return outs, ins

    def op(self, eng, fname, *args, **kw):
        if eng in self.DRAIN_ENGS and fname not in (None, "drain", "nop"):
            outs, ins = self._split_io(fname, args, kw)
            rds = [_ap_ranges(a) for a in ins + outs]
            hazard = any(_overlap(w, r) for w in self._written[eng] for r in rds)
            if hazard:
                self.q[eng].append(Op("drain", (), {}))
                self._written[eng] = []
            self._written[eng].extend(_ap_ranges(a) for a in outs)
        o = Op(fname, args, kw)
        self.q[eng].append(o)
        return o

    def wait(self, eng, sem, val=None):
        if val is None:
            val = sem.n
        if val <= 0:
            return
        lst = self.q[eng]
        o = Op(None, (), {})
        o.waits.append((sem, val))
        lst.append(o)

    def wait_buf(self, eng, buf):
        if buf.ready is None:
            return
        if isinstance(buf.ready, list):
            for s, v in buf.ready:
                self.wait(eng, s, v)
        else:
            self.wait(eng, *buf.ready)

    def free_buf(self, eng_unused, buf, sem, val=None):
        buf.frees.append((sem, sem.n if val is None else val))

    @staticmethod
    def _merge(frees):
        m = {}
        for s, v in frees:
            if id(s) not in m or m[id(s)][1] < v:
                m[id(s)] = (s, v)
        return list(m.values())

    def acquire(self, eng, buf):
        for s, v in self._merge(buf.frees):
            self.wait(eng, s, v)
        buf.frees = []
        buf.ready = None

    def acquire2(self, engs, buf):
        for e in engs:
            for s, v in self._merge(buf.frees):
                self.wait(e, s, v)
        buf.frees = []
        buf.ready = None

    def inc(self, eng, sem, n=1):
        lst = self.q[eng]
        assert lst and lst[-1].fname is not None, f"inc on {eng} with no op"
        lst[-1].incs.append((sem, n))
        sem.n += n

    def check(self):
        pcs = {e: 0 for e in ENGINES}
        vals = {id(s): 0 for s in self.sems.values()}
        total = sum(len(self.q[e]) for e in ENGINES)
        done = 0
        progress = True
        while progress:
            progress = False
            for e in ENGINES:
                while pcs[e] < len(self.q[e]):
                    o = self.q[e][pcs[e]]
                    if any(vals[id(s)] < v for s, v in o.waits):
                        break
                    for s, n in o.incs:
                        vals[id(s)] += n
                    pcs[e] += 1
                    done += 1
                    progress = True
        if done != total:
            msg = []
            for e in ENGINES:
                if pcs[e] < len(self.q[e]):
                    o = self.q[e][pcs[e]]
                    blocked = [(s.name, v, vals[id(s)]) for s, v in o.waits if vals[id(s)] < v]
                    msg.append(f"{e} stuck at {pcs[e]}/{len(self.q[e])} op={o.fname} waiting {blocked}")
            raise RuntimeError("DEADLOCK in planned program:\n" + "\n".join(msg))

    def emit(self, block):
        def runner(ops):
            def f(engine_obj):
                for o in ops:
                    for s, v in o.waits:
                        engine_obj.wait_ge(s.handle, v)
                    if o.fname is None:
                        continue
                    inst = getattr(engine_obj, o.fname)(*o.args, **o.kw)
                    for s, n in o.incs:
                        inst = inst.then_inc(s.handle, n)
            return f
        block.sync(runner(self.q["sync"]))
        block.vector(runner(self.q["vector"]))
        block.scalar(runner(self.q["scalar"]))
        block.tensor(runner(self.q["tensor"]))
        block.gpsimd(runner(self.q["gpsimd"]))


def stage_inputs(inp, cores, T):
    u = np.asarray(inp["L0_attn_u"])
    W_in = np.asarray(inp["L0_attn_W_in"])
    conv_w = np.asarray(inp["L0_attn_conv_w"])
    conv_b = np.asarray(inp["L0_attn_conv_b"])
    dt_bias = np.asarray(inp["L0_attn_dt_bias"])
    A_log = np.asarray(inp["L0_attn_A_log"])
    Dv = np.asarray(inp["L0_attn_D"])
    nw = np.asarray(inp["L0_attn_norm_weight"])
    W_out = np.asarray(inp["L0_attn_W_out"])
    gw = np.asarray(inp["L0_ffn_gate_w"])
    uw = np.asarray(inp["L0_ffn_up_w"])
    dw = np.asarray(inp["L0_ffn_down_w"])
    gs = np.asarray(inp["L0_ffn_gate_scale"])
    us = np.asarray(inp["L0_ffn_up_scale"])
    ds = np.asarray(inp["L0_ffn_down_scale"])

    w_gate = np.ascontiguousarray(gw.reshape(16, 128, 48, 128).transpose(2, 1, 0, 3))
    w_up = np.ascontiguousarray(uw.reshape(16, 128, 48, 128).transpose(2, 1, 0, 3))
    w_down = np.ascontiguousarray(dw.reshape(48, 128, 16, 128).transpose(2, 1, 0, 3))
    WiT = np.ascontiguousarray(W_in.T)
    w_inT = np.ascontiguousarray(WiT[:, :10240].reshape(16, 128, 80, 128).transpose(2, 1, 0, 3))
    w_indt = np.ascontiguousarray(WiT[:, 10240:].reshape(16, 128, 64).transpose(1, 0, 2))
    WoT = np.ascontiguousarray(W_out.T)
    w_outT = np.ascontiguousarray(WoT.reshape(32, 128, 16, 128).transpose(2, 1, 0, 3))

    v_convw = np.ascontiguousarray(conv_w.reshape(48, 128, 4).transpose(1, 0, 2).reshape(128, 192))
    v_convb = np.ascontiguousarray(conv_b.reshape(48, 128).T)
    v_dtb = np.ascontiguousarray(dt_bias.reshape(64, 1))
    v_alog = np.ascontiguousarray(A_log.reshape(64, 1))
    v_dpc = np.ascontiguousarray(
        np.where(np.arange(128)[:, None] < 64, Dv[0::2][None, :], Dv[1::2][None, :]))
    v_nwpc = np.ascontiguousarray(nw.reshape(32, 128).T)
    v_gs = np.ascontiguousarray(gs.reshape(48, 128).T)
    v_us = np.ascontiguousarray(us.reshape(48, 128).T)
    v_ds = np.ascontiguousarray(ds.reshape(16, 128).T)

    c_actconst = np.tile(np.array([[1e-5, -0.5, 1.0 / 128.0, 0.5, -1.0, 2.0, 0.0, 1.0]],
                                   np.float32), (128, 1))
    c_ident_bf = np.eye(128, dtype=BF)
    c_ident_f32 = np.eye(128, dtype=np.float32)
    c_ones_row = np.ones((1, 128), np.float32)
    c_ones_col_bf = np.ones((128, 1), BF)
    idx = np.arange(128)
    c_ule = (idx[:, None] <= idx[None, :]).astype(np.float32)
    c_slt = (idx[:, None] > idx[None, :]).astype(np.float32)
    c_t01 = (idx[None, :] >= idx[:, None]).astype(np.float32)
    c_uleb = c_ule.astype(BF)
    c_sltb = c_slt.astype(BF)
    c_maskneg = (-1.0e9 * (idx[None, :] < idx[:, None])).astype(BF)
    c_ones128b = np.ones((128, 128), BF)
    c_e0b = np.zeros((128, 128), BF)
    c_e0b[0, :] = 1
    c_onesrowb = np.ones((1, 128), BF)
    c_ones128f = np.ones((128, 128), np.float32)

    c_zeros = np.zeros((128, 1024), np.float32)
    shared = dict(
        w_gate=w_gate, w_up=w_up, w_down=w_down, w_inT=w_inT, w_indt=w_indt,
        w_outT=w_outT, v_convw=v_convw, v_convb=v_convb, v_dtb=v_dtb,
        v_alog=v_alog, v_dpc=v_dpc.astype(np.float32), v_nwpc=v_nwpc,
        v_gs=v_gs, v_us=v_us, v_ds=v_ds,
        c_actconst=c_actconst,
        c_ident_bf=c_ident_bf, c_ident_f32=c_ident_f32, c_ones_row=c_ones_row,
        c_ones_col_bf=c_ones_col_bf, c_t01=c_t01,
        c_zeros=c_zeros, c_uleb=c_uleb, c_sltb=c_sltb, c_maskneg=c_maskneg,
        c_ones128b=c_ones128b, c_e0b=c_e0b,
    )
    in_maps = []
    for k in cores:
        s, h = k // 2, k % 2
        m = dict(shared)
        m["u_fm"] = np.ascontiguousarray(u[s, h * T:(h + 1) * T, :].T)
        m["gain"] = np.full((128, 1), 1.0 if h == 1 else 0.0, np.float32)
        in_maps.append(m)
    return in_maps


class Kern:
    def __init__(self, T=2048, pair_groups=None, stop_after=None, dbg_ssd=False):
        assert T % 512 == 0
        self.T = T
        self.NMB = T // 512
        self.NCH = T // CK
        self.FFN_BLKS = [1024] * (T // 1024) if T % 1024 == 0 else [512] * (T // 512)
        self.pair_groups = pair_groups or [[0, 1], [2, 3], [4, 5], [6, 7]]
        self.stop_after = stop_after
        self.dbg_ssd = dbg_ssd
        self.p = Prog()
        self.nc = bass.Bass("TRN2", target_bir_lowering=False, debug=False)
        self._build()

    def _alloc(self):
        nc = self.nc
        self._ctx = []

        def sb(name, shape, dt=F32):
            g = nc.sbuf_tensor(name, shape, dt)
            t = g.__enter__()
            self._ctx.append(g)
            return t

        T = self.T
        din = {}

        def dr_in(name, shape, dt=F32):
            din[name] = nc.dram_tensor(name, list(shape), dt, kind="ExternalInput").ap()

        dr_in("u_fm", (D, T))
        dr_in("w_gate", (48, 128, 16, 128)); dr_in("w_up", (48, 128, 16, 128))
        dr_in("w_down", (16, 128, 48, 128))
        dr_in("w_inT", (80, 128, 16, 128)); dr_in("w_indt", (128, 16, 64))
        dr_in("w_outT", (16, 128, 32, 128))
        dr_in("v_convw", (128, 48 * 4)); dr_in("v_convb", (128, 48))
        dr_in("v_dtb", (64, 1)); dr_in("v_alog", (64, 1))
        dr_in("v_dpc", (128, 32)); dr_in("v_nwpc", (128, 32))
        dr_in("v_gs", (128, 48)); dr_in("v_us", (128, 48)); dr_in("v_ds", (128, 16))
        dr_in("c_actconst", (128, 8))
        dr_in("c_ident_bf", (128, 128), BF16); dr_in("c_ident_f32", (128, 128))
        dr_in("c_ones_row", (1, 128)); dr_in("c_ones_col_bf", (128, 1), BF16)
        dr_in("c_t01", (128, 128))
        dr_in("c_zeros", (128, 1024))
        dr_in("c_uleb", (128, 128), BF16); dr_in("c_sltb", (128, 128), BF16)
        dr_in("c_e0b", (128, 128), BF16)
        dr_in("c_maskneg", (128, 128), BF16); dr_in("c_ones128b", (128, 128), BF16)
        dr_in("gain", (128, 1))
        self.din = din

        self.d_out = nc.dram_tensor("out_fm", [D, T], F32, kind="ExternalOutput").ap()

        def dr_s(name, shape, dt=F32):
            return nc.dram_tensor(name, list(shape), dt, kind="Internal").ap()

        self.d_r1 = dr_s("sp_r1", (D, T))
        self.d_r2 = dr_s("sp_r2", (D, T))
        self.d_w8gu = dr_s("wc8_gu", (48, 128, 4096), F8)
        self.d_w8d = dr_s("wc8_down", (16, 128, 6144), F8)
        self.d_wbin = dr_s("wcb_in", (81, 128, 2048), BF16)
        self.d_wbout = dr_s("wcb_out", (16, 128, 4096), BF16)
        nmb = self.NMB
        self.d_sz = dr_s("sp_sz", (128, nmb * NPAIR * 512), BF16)
        self.d_c = dr_s("sp_c", (128, nmb * NPAIR * 512), BF16)
        self.d_yp = dr_s("sp_yp", (128, nmb * NPAIR * 512))
        self.d_halo_in = dr_s("cc_halo_in", (D, 4))
        self.d_halo_out = dr_s("cc_halo_out", (2 * D, 4))
        self.d_carry_snd = dr_s("cc_carry_snd", (128, 4096))
        self.d_carry_rcv = dr_s("cc_carry_rcv", (2 * 128, 4096))

        self.ARN = sb("arena", [128, 65536], BF16)

        DTCS = sb("dtcs", [128, T])
        self.DTF = DTCS[0:64, :]
        self.CSF = DTCS[64:128, :]
        self.CLOG = sb("clog", [1, self.NCH * 64])
        self.PREV = self.ARN[:, 49152 + 16 * 512:49152 + 16 * 512 + 8192].bitcast(F32)
        self.CARRY = None
        self.WR = sb("wring", [128, 2 * 6144], BF16)
        self.IOB = sb("iob", [128, 2 * 1024])
        self.STG = sb("stg", [128, 2 * 2048], BF16)
        self.ROWS = sb("rows", [1, 3 * 1024])
        self.HALO = sb("halo", [128, 16 * 3])
        self.XBCH = sb("xbch", [128, 48 * 3])
        self.TAIL = sb("tail", [128, 48 * 3])

        self.ACONST = sb("aconst", [128, 8])
        self.IDB = sb("idb", [128, 128], BF16)
        self.IDF = sb("idf", [128, 128])
        self.ONR = sb("onr", [1, 128])
        self.ONCB = sb("oncb", [128, 1], BF16)
        self.T01 = sb("t01", [128, 128])
        self.ULEB = sb("uleb", [128, 128], BF16)
        self.SLTB = sb("sltb", [128, 128], BF16)
        self.E0B = sb("e0b", [128, 128], BF16)
        self.MASKN = sb("maskn", [128, 128], BF16)
        self.ONES128B = sb("ones128b", [128, 128], BF16)
        self.DAHL = sb("dahl", [128, 2 * self.NCH * 64], BF16)
        self.CSTN = sb("cstn", [128, max(self.NCH * 64, 1024)])
        self.F8V = self.CSTN[:, 0:1024].bitcast(F8)
        self.CONVW = sb("convw", [128, 48 * 4])
        self.CONVB = sb("convb", [128, 48])
        self.DTBias = sb("dtb", [64, 1])
        self.ALOG = sb("alog", [64, 1])
        self.ACOL = sb("acol", [64, 1])
        self.DPC = sb("dpc", [128, 32])
        self.NWPC = sb("nwpc", [128, 32])
        self.GSC = sb("gsc", [128, 48])
        self.USC = sb("usc", [128, 48])
        self.DSC = sb("dsc", [128, 16])
        self.DSH = sb("dsh", [128, 16])
        self.GAIN = sb("gainp", [128, 1])

        self.PSA = self._psalloc("psA")
        self.PSB = self._psalloc("psB")
        self.PSC = self._psalloc("psC")
        self.PSD = self._psalloc("psD")

    def _psalloc(self, name):
        g = self.nc.psum_tensor(name, [128, 1024], F32)
        t = g.__enter__()
        self._ctx.append(g)
        return t

    def op(self, *a, **k): return self.p.op(*a, **k)
    def wait(self, *a, **k): return self.p.wait(*a, **k)
    def inc(self, *a, **k): return self.p.inc(*a, **k)

    def dma(self, eng, out, in_, sem, **kw):
        self.op(eng, "dma_start", out=out, in_=in_, **kw)
        if isinstance(sem, Buf):
            self.inc(eng, sem.sem, 16)
            sem.ready = (sem.sem, sem.sem.n)
        else:
            self.inc(eng, sem, 16)

    def wcast(self, cache, in_, parity):
        b = self.b_c8[parity]
        self.op("gpsimd", "dma_start", out=cache, in_=in_)
        self.inc("gpsimd", b.sem, 16)
        return (b.sem, b.sem.n)

    def dma_q8(self, out, in_, buf, cache=None):
        n = out.shape[1]
        if isinstance(buf.ready, list):
            ready = buf.ready
        elif buf.ready is not None:
            ready = [buf.ready]
        else:
            ready = []
        for off in range(0, n, 2048):
            m = min(2048, n - off)
            sl = self._f8i % 2
            self._f8i += 1
            s = self.s_f8s[sl]
            f8 = self.F8V[:, sl * 2048: sl * 2048 + m]
            self.wait("gpsimd", s, s.n)
            self.op("gpsimd", "dma_start", out=f8, in_=in_[:, off:off + m])
            self.inc("gpsimd", s, 16)
            self.wait("gpsimd", s, s.n)
            if cache is not None:
                self.op("gpsimd", "dma_start", out=cache[:, off:off + m], in_=f8)
                self.inc("gpsimd", s, 16)
                self.wait("gpsimd", s, s.n)
            self.op("gpsimd", "dma_start", out=out[:, off:off + m], in_=f8)
            self.inc("gpsimd", s, 16)
            ready.append((s, s.n))
        buf.ready = ready

    def schain(self):
        self.inc("scalar", self.s_s, 1)
        self.wait("scalar", self.s_s, self.s_s.n)

    def barrier(self, tag):
        p = self.p
        s = p.sem(f"bar_{tag}")
        for e in ENGINES:
            self.op(e, "nop")
            self.inc(e, s, 1)
        for e in ENGINES:
            self.wait(e, s, 5 if False else len(ENGINES))

    def _build(self):
        self._alloc()
        p = self.p
        self.s_ld = p.sem("s_ld")
        self.s_w = p.sem("s_w")
        self.s_x = p.sem("s_x")
        self.s_sp = p.sem("s_sp")
        self.s_t = p.sem("s_t")
        self.s_v = p.sem("s_v")
        self.s_s = p.sem("s_s")
        self.s_cc = p.sem("s_cc")
        self.s_out = p.sem("s_out")

        self.b_iob = [Buf(self.p), Buf(self.p)]
        self.b_stg = [Buf(self.p), Buf(self.p)]
        self.b_wr = [Buf(self.p) for _ in range(6)]
        self.b_wst = [Buf(self.p) for _ in range(6)]
        self._c_ffn = False
        self.b_c8 = [Buf(self.p), Buf(self.p)]
        self._castv = {}
        self._c_inp = False
        self._c_outp = False
        self.b_ps = {"A": Buf(), "B": Buf(), "C": Buf(), "D": Buf()}
        self.s_spc = self.p.sem("s_spc")
        self.s_f8 = self.p.sem("s_f8")
        self.s_f8s = [self.p.sem("s_f8a"), self.p.sem("s_f8b")]
        self._f8i = 0
        self.b_halo = Buf(self.p)
        self.b_clog = Buf(self.p)
        self.s_spz = self.p.sem("s_spz")

        self._load_consts()
        self._ffn(x_hbm=self.din["u_fm"], dst_hbm=self.d_r1, collect_halo=True)
        if self.stop_after == "ffn1":
            self._finalize_dev(copy_from=self.d_r1)
            self._emit()
            return
        self._mamba_front()
        if self.stop_after == "front":
            self._dev_dump([("dbg_yp", self.d_yp, [128, self.NCH * 4096]),
                            ("dbg_sz", self.d_sz, [128, self.NMB * NPAIR * 512]),
                            ("dbg_carry", self.d_carry_rcv, [256, 4096]),
                            ("dbg_dr1", self.d_r1, [D, self.T]),
                            ("dbg_c", self.d_c, [128, self.NMB * NPAIR * 512])])
            self._emit()
            return
        self._mamba_back()
        if self.stop_after == "back":
            self._finalize_dev(copy_from=self.d_r2)
            self._emit()
            return
        self._ffn(x_hbm=self.d_r2, dst_hbm=self.d_out, collect_halo=False)
        self.wait("sync", self.b_iob[0].sem, self.b_iob[0].sem.n)
        self.wait("sync", self.b_iob[1].sem, self.b_iob[1].sem.n)
        self._emit()

    def _emit(self):
        self.p.check()
        nc = self.nc
        semctx = []
        for s in self.p.sems.values():
            g = nc.semaphore(s.name)
            s.handle = g.__enter__()
            semctx.append(g)
        blk = nc.Block()
        b = blk.__enter__()
        self.p.emit(b)
        blk.__exit__(None, None, None)
        for g in semctx:
            g.__exit__(None, None, None)
        for g in reversed(self._ctx):
            g.__exit__(None, None, None)

    def _dbg_snap(self, name, src):
        dst = self.nc.dram_tensor(name, list(src.shape), src.dtype,
                                  kind="ExternalOutput").ap()
        b = Buf(self.p)
        self.wait("sync", self.s_v, self.s_v.n)
        self.wait("sync", self.s_t, self.s_t.n)
        self.wait("sync", self.s_s, self.s_s.n)
        self.dma("sync", dst, src, b)
        for e in ("vector", "scalar", "tensor"):
            self.p.wait_buf(e, b)

    def _dev_dump(self, specs):
        for s in list(self.p.sems.values()):
            if s.n > 0:
                self.wait("sync", s, s.n)
        for name, src_ap, shape in specs:
            dst = self.nc.dram_tensor(name, list(shape), src_ap.dtype,
                                      kind="ExternalOutput").ap()
            nrow, ncol = shape
            for r0 in range(0, nrow, 128):
                rows = min(128, nrow - r0)
                for c0 in range(0, ncol, 2048):
                    cols = min(2048, ncol - c0)
                    if src_ap.dtype == BF16:
                        buf = self.STG[0:rows, 0:cols]
                    else:
                        buf = self.IOB[0:rows, 0:cols]
                    self.dma("sync", buf, src_ap[r0:r0 + rows, c0:c0 + cols], self.s_x)
                    self.wait("sync", self.s_x, self.s_x.n)
                    self.dma("sync", dst[r0:r0 + rows, c0:c0 + cols], buf, self.s_out)
                    self.wait("sync", self.s_out, self.s_out.n)

    def _finalize_dev(self, copy_from):
        for k in range(16):
            slot = k % 2
            buf = self.IOB[:, slot * 1024:(slot + 1) * 1024]
            b = self.b_iob[slot]
            nh = self.T // 1024 if self.T >= 1024 else 1
            w = min(1024, self.T)
            for h in range(nh):
                self.p.acquire("sync", b)
                self.dma("sync", buf[:, 0:w],
                         copy_from[k * 128:(k + 1) * 128, h * w:(h + 1) * w], b)
                self.p.wait_buf("sync", b)
                self.dma("sync", self.d_out[k * 128:(k + 1) * 128, h * w:(h + 1) * w],
                         buf[:, 0:w], b)
                b.frees.append((b.sem, b.sem.n))
        self.wait("sync", self.s_out, self.s_out.n)

    def _load_consts(self):
        d = self.din
        ld = self.s_ld
        for dst, src in [
            (self.ACONST, "c_actconst"),
            (self.IDB, "c_ident_bf"), (self.IDF, "c_ident_f32"),
            (self.ONR, "c_ones_row"), (self.ONCB, "c_ones_col_bf"),
            (self.T01, "c_t01"),
            (self.ULEB, "c_uleb"), (self.SLTB, "c_sltb"), (self.E0B, "c_e0b"),
            (self.MASKN, "c_maskneg"),
            (self.ONES128B, "c_ones128b"),
            (self.CONVW, "v_convw"), (self.CONVB, "v_convb"),
            (self.DTBias, "v_dtb"), (self.ALOG, "v_alog"),
            (self.DPC, "v_dpc"), (self.NWPC, "v_nwpc"),
            (self.GSC, "v_gs"), (self.USC, "v_us"), (self.DSC, "v_ds"),
            (self.GAIN, "gain"),
        ]:
            self.dma("sync", dst[:, :], d[src][:, :], ld)
        self.wait("scalar", ld, ld.n)
        self.op("scalar", "activation", self.ACOL[:, :], self.ALOG[:, :], AF.Exp,
                bias=0.0, scale=1.0)
        self.inc("scalar", self.s_s, 1)
        self.wait("vector", ld, ld.n)
        self.wait("vector", self.s_s, self.s_s.n)
        self.op("vector", "tensor_scalar", self.ACOL[:, :], self.ACOL[:, :], -1.0, None, ALU.mult)
        self.op("vector", "tensor_scalar", self.DSH[:, :], self.DSC[:, :], 0.5, None, ALU.mult)
        self.inc("vector", self.s_v, 1)
        for e in ("tensor", "gpsimd", "sync"):
            self.wait(e, ld, ld.n)
            self.wait(e, self.s_v, self.s_v.n)

    def _layernorm_to_xn(self, x_hbm, tok0, TB, XN):
        nck = TB // 512
        for nm in ("A", "B"):
            self.p.acquire("tensor", self.b_ps[nm])
        for k in range(16):
            slot = k % 2
            xin = self.IOB[:, slot * 1024: slot * 1024 + TB]
            b = self.b_iob[slot]
            self.p.acquire("sync", b)
            self.dma("sync", xin, x_hbm[k * 128:(k + 1) * 128, tok0:tok0 + TB], b)
            bs = self.b_stg[slot]
            self.p.acquire2(("vector", "scalar"), bs)
            self.p.wait_buf("vector", b)
            xb = self.STG[:, slot * 2048: slot * 2048 + TB]
            self.op("vector", "tensor_copy", xb, xin)
            self.inc("vector", self.s_v, 1)
            sq = self.STG[:, slot * 2048 + 1024: slot * 2048 + 1024 + TB]
            self.p.wait_buf("scalar", b)
            self.op("scalar", "activation", sq, xin, AF.Square, bias=0.0, scale=1.0)
            self.inc("scalar", self.s_s, 1)
            self.p.free_buf(None, b, self.s_v)
            self.p.free_buf(None, b, self.s_s)
            self.wait("tensor", self.s_v, self.s_v.n)
            self.wait("tensor", self.s_s, self.s_s.n)
            for h in range(nck):
                self.op("tensor", "matmul", self.PSA[0:1, h * 512:(h + 1) * 512],
                        self.ONCB[:, :], xb[:, h * 512:(h + 1) * 512],
                        start=(k == 0), stop=(k == 15))
                self.op("tensor", "matmul", self.PSB[0:1, h * 512:(h + 1) * 512],
                        self.ONCB[:, :], sq[:, h * 512:(h + 1) * 512],
                        start=(k == 0), stop=(k == 15))
            self.inc("tensor", self.s_t, 1)
            self.p.free_buf(None, bs, self.s_t)
        rows = self.ROWS
        mu = rows[0:1, 0:TB]
        msq = rows[0:1, 1024:1024 + TB]
        var = msq
        rstd = rows[0:1, 2048:2048 + TB]
        self.wait("vector", self.s_t, self.s_t.n)
        self.op("vector", "tensor_scalar", mu, self.PSA[0:1, 0:TB], 1.0 / D, None, ALU.mult)
        self.inc("vector", self.s_v, 1)
        self.wait("scalar", self.s_v, self.s_v.n)
        self.op("scalar", "activation", msq, mu, AF.Square, bias=0.0, scale=1.0)
        self.inc("scalar", self.s_s, 1)
        self.wait("vector", self.s_s, self.s_s.n)
        self.op("vector", "scalar_tensor_tensor", var, self.PSB[0:1, 0:TB], 1.0 / D, msq,
                ALU.mult, ALU.subtract)
        self.inc("vector", self.s_v, 1)
        self.b_ps["A"].frees.append((self.s_v, self.s_v.n))
        self.b_ps["B"].frees.append((self.s_v, self.s_v.n))
        self.wait("scalar", self.s_v, self.s_v.n)
        self.op("scalar", "activation", var, var, AF.Ln, bias=self.ACONST[0:1, 0:1], scale=1.0)
        self.schain()
        self.op("scalar", "activation", rstd, var, AF.Exp, bias=0.0,
                scale=self.ACONST[0:1, 1:2])
        self.inc("scalar", self.s_s, 1)
        for nm in ("C", "D"):
            self.p.acquire("tensor", self.b_ps[nm])
        self.wait("tensor", self.s_s, self.s_s.n)
        for h in range(nck):
            self.op("tensor", "matmul", self.PSC[:, h * 512:(h + 1) * 512],
                    self.ONR[:, :], mu[0:1, h * 512:(h + 1) * 512], start=True, stop=True)
            self.op("tensor", "matmul", self.PSD[:, h * 512:(h + 1) * 512],
                    self.ONR[:, :], rstd[0:1, h * 512:(h + 1) * 512], start=True, stop=True)
        self.inc("tensor", self.s_t, 1)
        t_bc = self.s_t.n
        self._v_xn_chunk = [0] * 16
        for k in range(16):
            slot = k % 2
            xin = self.IOB[:, slot * 1024: slot * 1024 + TB]
            b = self.b_iob[slot]
            self.p.acquire("sync", b)
            self.dma("sync", xin, x_hbm[k * 128:(k + 1) * 128, tok0:tok0 + TB], b)
            self.p.wait_buf("vector", b)
            if k == 0:
                self.wait("vector", self.s_t, t_bc)
            xnk = XN[:, k * TB:(k + 1) * TB]
            self.op("vector", "tensor_tensor", xnk, xin, self.PSC[:, 0:TB], ALU.subtract)
            self.op("vector", "tensor_tensor", xnk, xnk, self.PSD[:, 0:TB], ALU.mult)
            self.inc("vector", self.s_v, 1)
            self._v_xn_chunk[k] = self.s_v.n
            self.p.free_buf(None, b, self.s_v)
        self.b_ps["C"].frees.append((self.s_v, self.s_v.n))
        self.b_ps["D"].frees.append((self.s_v, self.s_v.n))
        return self.s_v.n

    def _ffn(self, x_hbm, dst_hbm, collect_halo):
        T = self.T
        if not collect_halo:
            for e in ENGINES:
                self.wait(e, self.s_t, self.s_t.n)
                self.wait(e, self.s_v, self.s_v.n)
                self.wait(e, self.s_s, self.s_s.n)
        tok0 = 0
        for bi, TB in enumerate(self.FFN_BLKS):
            self._ffn_block(x_hbm, dst_hbm, tok0, TB,
                            halo=(collect_halo and tok0 + TB == T))
            tok0 += TB
        if collect_halo:
            self._halo_exchange()

    def _ffn_block(self, x_hbm, dst_hbm, tok0, TB, halo):
        nck = TB // 512
        FUSE = self.ARN[:, 0:48 * 1024]
        XN = self.ARN[:, 49152:49152 + 16 * TB]
        v_xn = self._layernorm_to_xn(x_hbm, tok0, TB, XN)

        def gu_cast(jj):
            self.wcast(self.d_w8gu[jj][:, 0:2048],
                       self.din["w_gate"][jj].rearrange("f k d -> f (k d)"), jj % 2)
            self._castv[("gu", jj)] = self.wcast(
                self.d_w8gu[jj][:, 2048:4096],
                self.din["w_up"][jj].rearrange("f k d -> f (k d)"), jj % 2)
        for b in self.b_wr:
            self.p.acquire("gpsimd", b)
        for j in range(48):
            wslot = j % 3
            slot = j % 2
            wgu = self.WR[:, wslot * 4096: wslot * 4096 + 4096]
            bw = self.b_wr[wslot]
            self.p.acquire("gpsimd", bw)
            if not self._c_ffn:
                if j == 0:
                    gu_cast(0)
                if j + 1 < 48:
                    gu_cast(j + 1)
            self.wait("gpsimd", *self._castv[("gu", j)])
            self.dma("gpsimd", wgu, self.d_w8gu[j], bw)
            wgt = wgu[:, 0:2048]
            wut = wgu[:, 2048:4096]
            psg, psu = ("A", "B") if slot == 0 else ("C", "D")
            pg = {"A": self.PSA, "B": self.PSB, "C": self.PSC, "D": self.PSD}
            self.p.acquire("tensor", self.b_ps[psg])
            self.p.acquire("tensor", self.b_ps[psu])
            self.p.wait_buf("tensor", bw)
            for k in range(16):
                if j == 0:
                    self.wait("tensor", self.s_v, self._v_xn_chunk[k])
                lg = wgt[:, k * 128:(k + 1) * 128]
                lu = wut[:, k * 128:(k + 1) * 128]
                for h in range(nck):
                    rhs = XN[:, k * TB + h * 512: k * TB + h * 512 + 512]
                    self.op("tensor", "matmul", pg[psg][:, h * 512:(h + 1) * 512],
                            lg, rhs, start=(k == 0), stop=(k == 15))
                    self.op("tensor", "matmul", pg[psu][:, h * 512:(h + 1) * 512],
                            lu, rhs, start=(k == 0), stop=(k == 15))
            self.inc("tensor", self.s_t, 1)
            bw.frees.append((self.s_t, self.s_t.n))
            bs = self.b_stg[slot]
            self.p.acquire2(("scalar", "vector"), bs)
            self.wait("scalar", self.s_t, self.s_t.n)
            gt = self.STG[:, slot * 2048: slot * 2048 + TB]
            self.op("scalar", "activation", gt, pg[psg][:, 0:TB], AF.Silu,
                    bias=0.0, scale=self.GSC[:, j:j + 1])
            self.inc("scalar", self.s_s, 1)
            ut = self.STG[:, slot * 2048 + 1024: slot * 2048 + 1024 + TB]
            self.wait("vector", self.s_t, self.s_t.n)
            self.op("vector", "tensor_scalar", ut, pg[psu][:, 0:TB],
                    self.USC[:, j:j + 1], None, ALU.mult)
            self.wait("vector", self.s_s, self.s_s.n)
            self.op("vector", "tensor_tensor", FUSE[:, j * 1024: j * 1024 + TB], gt, ut, ALU.mult)
            self.inc("vector", self.s_v, 1)
            bs.frees.append((self.s_v, self.s_v.n))
            self.b_ps[psg].frees.append((self.s_s, self.s_s.n))
            self.b_ps[psu].frees.append((self.s_v, self.s_v.n))
        v_fuse = self.s_v.n

        def dn_cast(mm):
            self._castv[("dn", mm)] = self.wcast(
                self.d_w8d[mm],
                self.din["w_down"][mm].rearrange("f c d -> f (c d)"), mm % 2)
        for b in self.b_wr:
            self.p.acquire("gpsimd", b)
        for m in range(16):
            slot = m % 2
            wslot = self.WR[:, slot * 6144:(slot + 1) * 6144]
            bw = self.b_wr[slot]
            self.p.acquire("gpsimd", bw)
            if not self._c_ffn:
                if m == 0:
                    dn_cast(0)
                if m + 1 < 16:
                    dn_cast(m + 1)
            self.wait("gpsimd", *self._castv[("dn", m)])
            self.dma("gpsimd", wslot[:, 0:6144], self.d_w8d[m], bw)
            psd = "A" if slot == 0 else "C"
            pg = {"A": self.PSA, "C": self.PSC}
            self.p.acquire("tensor", self.b_ps[psd])
            xres = self.IOB[:, slot * 1024: slot * 1024 + TB]
            b2 = self.b_iob[slot]
            self.p.acquire("sync", b2)
            self.dma("sync", xres, x_hbm[m * 128:(m + 1) * 128, tok0:tok0 + TB], b2)
            self.p.wait_buf("tensor", bw)
            if m == 0:
                self.wait("tensor", self.s_v, v_fuse)
            for c in range(48):
                lw = wslot[:, c * 128:(c + 1) * 128]
                for h in range(nck):
                    self.op("tensor", "matmul", pg[psd][:, h * 512:(h + 1) * 512],
                            lw, FUSE[:, c * 1024 + h * 512: c * 1024 + h * 512 + 512],
                            start=(c == 0), stop=(c == 47))
            self.inc("tensor", self.s_t, 1)
            bw.frees.append((self.s_t, self.s_t.n))
            self.wait("vector", self.s_t, self.s_t.n)
            self.p.wait_buf("vector", b2)
            rout = xres
            self.op("vector", "scalar_tensor_tensor", rout, pg[psd][:, 0:TB],
                    self.DSH[:, m:m + 1], xres, ALU.mult, ALU.add)
            self.inc("vector", self.s_v, 1)
            b2.ready = (self.s_v, self.s_v.n)
            self.b_ps[psd].frees.append((self.s_v, self.s_v.n))
            self.p.wait_buf("sync", b2)
            self.dma("sync", dst_hbm[m * 128:(m + 1) * 128, tok0:tok0 + TB], rout, b2)
            b2.frees.append((b2.sem, b2.sem.n))
            if halo:
                self.dma("sync", self.d_halo_in[m * 128:(m + 1) * 128, 0:4],
                         rout[:, TB - 4:TB], b2)
                b2.frees.append((b2.sem, b2.sem.n))
        self._c_ffn = True

    def _halo_exchange(self):
        self.wait("gpsimd", self.b_iob[0].sem, self.b_iob[0].sem.n)
        self.wait("gpsimd", self.b_iob[1].sem, self.b_iob[1].sem.n)
        self.op("gpsimd", "collective_compute", "AllGather", ALU.bypass,
                replica_groups=self.pair_groups,
                ins=[self.d_halo_in.opt()], outs=[self.d_halo_out.opt()])
        self.inc("gpsimd", self.s_cc, 1)
        self.wait("sync", self.s_cc, self.s_cc.n)
        self.dma("sync", self.HALO[:, 0:16 * 3].rearrange("p (k c) -> p k c", c=3),
                 self.d_halo_out[0:D, 1:4].rearrange("(k p) c -> p k c", p=128), self.b_halo)
        self.p.wait_buf("vector", self.b_halo)
        self.op("vector", "tensor_scalar", self.HALO[:, 0:48], self.HALO[:, 0:48],
                self.GAIN[:, 0:1], None, ALU.mult)
        self.inc("vector", self.s_v, 1)
        self.v_halo = self.s_v.n



    @staticmethod
    def apx(base2d, dims):
        ap = [list(base2d.ap[0])] + [[s, c] for (s, c) in dims]
        return bass.AP(base2d.tensor, base2d.offset, ap)

    def _halo_ln(self):
        bs = self.b_stg[0]
        self.p.acquire2(("vector", "scalar"), bs)
        self.wait("vector", self.s_v, self.v_halo)
        hb = self.STG[:, 0:48]
        self.op("vector", "tensor_copy", hb, self.HALO[:, 0:48])
        self.inc("vector", self.s_v, 1)
        sqh = self.STG[:, 64:112]
        self.wait("scalar", self.s_v, self.v_halo)
        self.op("scalar", "activation", sqh, self.HALO[:, 0:48], AF.Square, bias=0.0, scale=1.0)
        self.inc("scalar", self.s_s, 1)
        self.p.acquire("tensor", self.b_ps["A"])
        self.p.acquire("tensor", self.b_ps["B"])
        self.wait("tensor", self.s_v, self.s_v.n)
        self.wait("tensor", self.s_s, self.s_s.n)
        for k in range(16):
            self.op("tensor", "matmul", self.PSA[0:1, 0:3], self.ONCB[:, :],
                    hb[:, 3 * k:3 * k + 3], start=(k == 0), stop=(k == 15))
            self.op("tensor", "matmul", self.PSB[0:1, 0:3], self.ONCB[:, :],
                    sqh[:, 3 * k:3 * k + 3], start=(k == 0), stop=(k == 15))
        self.inc("tensor", self.s_t, 1)
        bs.frees.append((self.s_t, self.s_t.n))
        rows = self.ROWS
        mu = rows[0:1, 0:3]
        msq = rows[0:1, 8:11]
        rstd = rows[0:1, 16:19]
        self.wait("vector", self.s_t, self.s_t.n)
        self.op("vector", "tensor_scalar", mu, self.PSA[0:1, 0:3], 1.0 / D, None, ALU.mult)
        self.inc("vector", self.s_v, 1)
        self.wait("scalar", self.s_v, self.s_v.n)
        self.op("scalar", "activation", msq, mu, AF.Square, bias=0.0, scale=1.0)
        self.inc("scalar", self.s_s, 1)
        self.wait("vector", self.s_s, self.s_s.n)
        self.op("vector", "scalar_tensor_tensor", msq, self.PSB[0:1, 0:3], 1.0 / D, msq,
                ALU.mult, ALU.subtract)
        self.inc("vector", self.s_v, 1)
        self.b_ps["A"].frees.append((self.s_v, self.s_v.n))
        self.b_ps["B"].frees.append((self.s_v, self.s_v.n))
        self.wait("scalar", self.s_v, self.s_v.n)
        self.op("scalar", "activation", msq, msq, AF.Ln, bias=self.ACONST[0:1, 0:1], scale=1.0)
        self.schain()
        self.op("scalar", "activation", rstd, msq, AF.Exp, bias=0.0, scale=self.ACONST[0:1, 1:2])
        self.inc("scalar", self.s_s, 1)
        self.p.acquire("tensor", self.b_ps["C"])
        self.p.acquire("tensor", self.b_ps["D"])
        self.wait("tensor", self.s_s, self.s_s.n)
        self.op("tensor", "matmul", self.PSC[:, 0:3], self.ONR[:, :], mu, start=True, stop=True)
        self.op("tensor", "matmul", self.PSD[:, 0:3], self.ONR[:, :], rstd, start=True, stop=True)
        self.inc("tensor", self.s_t, 1)
        self.wait("vector", self.s_t, self.s_t.n)
        xnh3 = self.XNH[:, 0:48].rearrange("p (k c) -> p k c", c=3)
        halo3 = self.HALO[:, 0:48].rearrange("p (k c) -> p k c", c=3)
        muB = self.PSC[:, 0:3].unsqueeze(1).broadcast_to([128, 16, 3])
        rsB = self.PSD[:, 0:3].unsqueeze(1).broadcast_to([128, 16, 3])
        self.op("vector", "tensor_tensor", xnh3, halo3, muB, ALU.subtract)
        self.op("vector", "tensor_tensor", xnh3, xnh3, rsB, ALU.mult)
        self.inc("vector", self.s_v, 1)
        self.b_ps["C"].frees.append((self.s_v, self.s_v.n))
        self.b_ps["D"].frees.append((self.s_v, self.s_v.n))
        self.v_xnh = self.s_v.n

    def _mamba_front(self):
        A = self.ARN
        XBC = A[:, 0:48 * 512]
        self.XBC = XBC
        XN = A[:, 24720:24720 + 8192]
        self.PREV = A[:, 32912:32912 + 8192].bitcast(F32)
        fr = dict(
            XBC=XBC, XN=XN,
            PREVB=A[:, 41104:43152],
            XT=A[:, 43152:44176],
            BT=A[:, 44176:45200],
            XDT=A[:, 45200:49296],
            DECT=A[:, 49296:49424].bitcast(F32),
            CDB=A[:, 49424:49552].bitcast(F32),
            EXPC0B=A[:, 49552:49680].bitcast(F32),
            E8=[A[:, 24720:26768].bitcast(F32),
                A[:, 26768:28816].bitcast(F32)],
            GL8=[A[:, 51728:52752], A[:, 50704:51728]],
            CHX8=[A[:, 52752:53776], A[:, 28816:29840]],
            XDTD8=[A[:, 53776:54288], A[:, 64528:65040]],
            ACC=A[:, 54288:55312].bitcast(F32),
            TSCR=A[:, 55312:56336].bitcast(F32),
            YST=A[:, 56336:56336 + 8192].bitcast(F32),
        )
        fr["DTRM"] = fr["ACC"]
        self.XNH = A[:, 50704:50704 + 64]
        fr["DAFM"] = A[:, 49680:49680 + 256].bitcast(F32)[0:64, 0:128]
        fr["ACCP"] = [fr["ACC"], fr["TSCR"]]
        self._fr = fr
        self.b_fr = {k: Buf(self.p) for k in ("TSCR", "DTRM", "YST")}
        self.b_e8 = [Buf(self.p), Buf(self.p)]
        self.b_gl8 = [Buf(self.p), Buf(self.p)]
        self.b_chx8 = [Buf(self.p), Buf(self.p)]
        self.b_xdtd = [Buf(self.p), Buf(self.p)]
        self.b_prevb = [Buf(self.p), Buf(self.p)]
        self.b_psby = Buf(self.p)
        self.b_psbx = Buf(self.p)
        self.b_pscg = Buf(self.p)
        self.b_pscs = Buf(self.p)
        for q in range(4):
            self.dma("sync", self.PREV[:, q * 1024:(q + 1) * 1024],
                     self.din["c_zeros"][:, :], self.s_x)
        self.v_prev0 = (self.s_x, self.s_x.n)

        self._halo_ln()
        for b in range(self.NMB):
            self._mamba_block(b)
        self.b_carry_out = Buf(self.p)
        self.wait("sync", self.s_v, self.s_v.n)
        self.dma("sync", self.d_carry_snd[:, :], self.PREV[:, :], self.b_carry_out)
        self.wait("gpsimd", self.b_carry_out.sem, self.b_carry_out.sem.n)
        self.op("gpsimd", "collective_compute", "AllGather", ALU.bypass,
                replica_groups=self.pair_groups,
                ins=[self.d_carry_snd.opt()], outs=[self.d_carry_rcv.opt()])
        self.inc("gpsimd", self.s_cc, 1)

    def _mamba_block(self, b):
        fr = self._fr
        tok0 = b * 512
        XN = fr["XN"]
        if b > 0:
            for e in ("scalar", "vector"):
                self.wait(e, self.s_t, self.s_t.n)
                self.wait(e, self.s_v, self.s_v.n)
                self.wait(e, self.s_spc, self.s_spc.n)
        v_xn = self._layernorm_to_xn(self.d_r1, tok0, 512, XN)

        psl = [self.PSA, self.PSB, self.PSC, self.PSD]
        psn = ["A", "B", "C", "D"]
        def in_cast(jj):
            wl = 1024 if jj == 80 else 2048
            s_ = (self.din["w_indt"].rearrange("f k d -> f (k d)") if jj == 80
                  else self.din["w_inT"][jj].rearrange("f k d -> f (k d)"))
            self._castv[("in", jj)] = self.wcast(self.d_wbin[jj][:, 0:wl], s_, jj % 2)
        for bwb in self.b_wr:
            self.p.acquire("gpsimd", bwb)
        for j in range(81):
            slot = j % 2
            wslot = j % 6
            is_dt = (j == 80)
            wlen = 1024 if is_dt else 2048
            wt = self.WR[:, wslot * 2048: wslot * 2048 + wlen]
            bw = self.b_wr[wslot]
            self.p.acquire("gpsimd", bw)
            if not self._c_inp:
                if j == 0:
                    in_cast(0)
                if j + 1 < 81:
                    in_cast(j + 1)
            self.wait("gpsimd", *self._castv[("in", j)])
            self.dma("gpsimd", wt, self.d_wbin[j][:, 0:wlen], bw)
            pb = psn[j % 4]
            ps = psl[j % 4]
            self.p.acquire("tensor", self.b_ps[pb])
            halo = (b == 0 and (32 <= j < 80))
            self.p.wait_buf("tensor", bw)
            if j == 0 and b == 0:
                self.wait("tensor", self.s_v, self.v_xnh)
            wd = 64 if is_dt else 128
            rows = 64 if is_dt else 128
            for k in range(16):
                if j == 0:
                    self.wait("tensor", self.s_v, self._v_xn_chunk[k])
                lw = wt[:, k * wd:(k + 1) * wd]
                self.op("tensor", "matmul", ps[0:rows, 0:512],
                        lw, XN[:, k * 512:(k + 1) * 512], start=(k == 0), stop=(k == 15))
                if halo:
                    self.op("tensor", "matmul", ps[0:rows, 512:515],
                            lw, self.XNH[:, k * 3:k * 3 + 3], start=(k == 0), stop=(k == 15))
            self.inc("tensor", self.s_t, 1)
            bw.frees.append((self.s_t, self.s_t.n))
            if j < 32:
                bs = self.b_stg[slot]
                self.p.acquire("scalar", bs)
                zt = self.STG[:, slot * 2048: slot * 2048 + 512]
                self.wait("scalar", self.s_t, self.s_t.n)
                self.op("scalar", "activation", zt, ps[:, 0:512], AF.Silu, bias=0.0, scale=1.0)
                self.inc("scalar", self.s_s, 1)
                self.b_ps[pb].frees.append((self.s_s, self.s_s.n))
                self.wait("sync", self.s_s, self.s_s.n)
                for c in range(4):
                    base = (b * 4 + c) * 4096 + j * 128
                    self.dma("sync", self.d_sz[:, base:base + 128],
                             zt[:, c * 128:(c + 1) * 128], bs)
                bs.frees.append((bs.sem, bs.sem.n))
            elif j < 80:
                t = j - 32
                if halo:
                    self.wait("vector", self.s_t, self.s_t.n)
                    self.op("vector", "tensor_copy", self.XBCH[:, t * 3:t * 3 + 3],
                            ps[:, 512:515])
                    self.inc("vector", self.s_v, 1)
                    self.b_ps[pb].frees.append((self.s_v, self.s_v.n))
                self._conv_tile(b, t, ps, pb)
            else:
                self.wait("vector", self.s_t, self.s_t.n)
                self.wait("vector", self.s_s, self.s_s.n)
                t0 = fr["ACC"][0:64, 0:512]
                t1 = fr["TSCR"][0:64, 0:512]
                self.p.acquire("vector", self.b_fr["TSCR"])
                self.op("vector", "tensor_scalar", t0, ps[0:64, 0:512],
                        self.DTBias[:, 0:1], None, ALU.add)
                self.inc("vector", self.s_v, 1)
                self.b_ps[pb].frees.append((self.s_v, self.s_v.n))
                self.wait("scalar", self.s_v, self.s_v.n)
                self.op("scalar", "activation", t1, t0, AF.Abs, bias=0.0, scale=1.0)
                self.schain()
                self.op("scalar", "activation", t1, t1, AF.Exp, bias=0.0, scale=-1.0)
                self.schain()
                self.op("scalar", "activation", t1, t1, AF.Ln, bias=1.0, scale=1.0)
                self.inc("scalar", self.s_s, 1)
                self.wait("vector", self.s_s, self.s_s.n)
                dtf = self.DTF[:, tok0:tok0 + 512]
                self.op("vector", "tensor_scalar", t0, t0, 0.0, None, ALU.max)
                self.op("vector", "tensor_tensor", dtf, t0, t1, ALU.add)
                self.inc("vector", self.s_v, 1)
                self.b_fr["TSCR"].frees.append((self.s_v, self.s_v.n))

        self._c_inp = True
        s_conv = self.s_s.n

        for c_loc in range(4):
            self._ssd_chunk(b, c_loc, s_conv)
        self._fr_psum_handoff()

    def _conv_tile(self, b, t, ps, pb):
        fr = self._fr
        tile = fr["XBC"][:, t * 512: t * 512 + 512]
        lead = self.XBCH[:, t * 3:t * 3 + 3] if b == 0 else self.TAIL[:, t * 3:t * 3 + 3]
        acc = fr["ACCP"][t % 2][:, 0:512]
        if t % 2 == 1:
            self.p.acquire("vector", self.b_fr["TSCR"])
        if t >= 2:
            self.wait("vector", self.s_s, self._conv_s[t % 2])
        self.wait("vector", self.s_t, self.s_t.n)
        self.op("vector", "tensor_scalar", acc, ps[:, 0:512],
                self.CONVW[:, 4 * t + 3:4 * t + 4], self.CONVB[:, t:t + 1],
                ALU.mult, ALU.add)
        for kk in range(3):
            off = 3 - kk
            self.op("vector", "scalar_tensor_tensor", acc[:, off:512],
                    ps[:, 0:512 - off],
                    self.CONVW[:, 4 * t + kk:4 * t + kk + 1], acc[:, off:512],
                    ALU.mult, ALU.add)
            self.op("vector", "scalar_tensor_tensor", acc[:, 0:off],
                    lead[:, kk:3],
                    self.CONVW[:, 4 * t + kk:4 * t + kk + 1], acc[:, 0:off],
                    ALU.mult, ALU.add)
        self.op("vector", "tensor_copy", self.TAIL[:, t * 3:t * 3 + 3], ps[:, 509:512])
        self.inc("vector", self.s_v, 1)
        self.b_ps[pb].frees.append((self.s_v, self.s_v.n))
        self.wait("scalar", self.s_v, self.s_v.n)
        self.op("scalar", "activation", tile[:, 0:512], acc, AF.Silu, bias=0.0, scale=1.0)
        self.inc("scalar", self.s_s, 1)
        if not hasattr(self, "_conv_s"):
            self._conv_s = {}
        self._conv_s[t % 2] = self.s_s.n
        if t % 2 == 1:
            self.b_fr["TSCR"].frees.append((self.s_s, self.s_s.n))
        if t >= 16:
            self.wait("sync", self.s_s, self.s_s.n)
            for c in range(4):
                base = (b * 4 + c) * 4096 + (t - 16) * 128
                self.dma("sync", self.d_c[:, base:base + 128],
                         tile[:, c * 128:(c + 1) * 128], self.s_spc)

    def _ssd_chunk(self, b, c_loc, s_conv):
        fr = self._fr
        ch = b * 4 + c_loc
        col0 = c_loc * 128
        XBC = fr["XBC"]

        def xbc_tile(t):
            return XBC[:, t * 512 + col0: t * 512 + col0 + 128]

        dafm = fr["DAFM"]
        self.op("vector", "tensor_scalar", dafm, self.DTF[:, ch * 128:(ch + 1) * 128],
                self.ACOL[:, 0:1], None, ALU.mult)
        self.inc("vector", self.s_v, 1)
        v_dafm = self.s_v.n
        self.p.acquire("tensor", self.b_ps["D"])
        self.wait("tensor", self.s_v, v_dafm)
        ps_daT = self.PSD[:, 512:576]
        ps_dtT = self.PSD[:, 576:640]
        self.op("tensor", "transpose", ps_daT, dafm, self.IDF[0:64, 0:64])
        self.op("tensor", "transpose", ps_dtT, self.DTF[:, ch * 128:(ch + 1) * 128],
                self.IDF[0:64, 0:64])
        self.inc("tensor", self.s_t, 1)
        t_tr = self.s_t.n
        dah = self.DAHL[:, ch * 64:(ch + 1) * 64]
        dal = self.DAHL[:, (self.NCH + ch) * 64:(self.NCH + ch + 1) * 64]
        self.wait("vector", self.s_t, t_tr)
        self.op("vector", "tensor_copy", dah, ps_daT)
        self.op("vector", "tensor_tensor", dal, ps_daT, dah, ALU.subtract)
        self.inc("vector", self.s_v, 1)
        v_dahl = self.s_v.n
        ps_cst = self.PSD[:, 640:704]
        ps_dec = self.PSD[:, 704:768]
        ps_cdf = self.PSD[:, 768:832]
        self.wait("tensor", self.s_v, v_dahl)
        for (lhs, pso) in ((self.ULEB, ps_cst), (self.SLTB, ps_dec), (self.ONES128B, ps_cdf)):
            self.op("tensor", "matmul", pso, lhs[:, :], dah, start=True, stop=False)
            self.op("tensor", "matmul", pso, lhs[:, :], dal, start=False, stop=True)
        self.inc("tensor", self.s_t, 1)
        t_mm = self.s_t.n
        cstn = self.CSTN[:, ch * 64:(ch + 1) * 64]
        self.wait("vector", self.s_t, t_mm)
        self.op("vector", "tensor_scalar", cstn, ps_cst, -1.0, None, ALU.mult)
        self.inc("vector", self.s_v, 1)
        v_cstn = self.s_v.n
        ps_cs0 = self.PSD[:, 832:896]
        self.op("tensor", "matmul", ps_cs0, self.E0B[:, :], dah, start=True, stop=False)
        self.op("tensor", "matmul", ps_cs0, self.E0B[:, :], dal, start=False, stop=True)
        self.inc("tensor", self.s_t, 1)
        t_mm = self.s_t.n
        if self.dbg_ssd and ch == 1:
            scr = fr["TSCR"][:, 0:64]
            self.wait("vector", self.s_t, t_mm)
            self.wait("vector", self.s_v, self.s_v.n)
            self.op("vector", "tensor_copy", scr, ps_cs0)
            self.inc("vector", self.s_v, 1)
            self._dbg_snap("dbg_cs0ps", scr)
        self.wait("scalar", self.s_t, t_mm)
        self.op("scalar", "activation", fr["DECT"][:, 0:64], ps_dec, AF.Exp, bias=0.0, scale=1.0)
        self.op("scalar", "activation", fr["CDB"][:, 0:64], ps_cdf, AF.Exp, bias=0.0, scale=1.0)
        self.op("scalar", "activation", fr["EXPC0B"][:, 0:64], ps_cs0, AF.Exp, bias=0.0, scale=1.0)
        self.inc("scalar", self.s_s, 1)
        s_decb = self.s_s.n
        self.wait("sync", self.s_v, v_cstn)
        self.dma("sync", self.CLOG[0:1, ch * 64:(ch + 1) * 64], cstn[127:128, :], self.b_clog)
        psd6 = self.PSD[:, 0:512].bitcast(BF16)
        self.wait("tensor", self.s_s, s_conv)
        for g in range(8):
            self.op("tensor", "transpose", psd6[:, g * 128:(g + 1) * 128],
                    xbc_tile(g), self.IDB[:, :])
        self.inc("tensor", self.s_t, 1)
        self.wait("vector", self.s_t, self.s_t.n)
        self.op("vector", "tensor_copy", fr["XT"][:, :], psd6[:, :])
        self.inc("vector", self.s_v, 1)
        self.wait("tensor", self.s_v, self.s_v.n)
        for g in range(8):
            self.op("tensor", "transpose", psd6[:, g * 128:(g + 1) * 128],
                    xbc_tile(8 + g), self.IDB[:, :])
        self.inc("tensor", self.s_t, 1)
        self.wait("vector", self.s_t, self.s_t.n)
        self.op("vector", "tensor_copy", fr["BT"][:, :], psd6[:, :])
        self.inc("vector", self.s_v, 1)
        v_xbt = self.s_v.n
        xt_rep = self.apx(fr["XT"][:, 0:1024], [(128, 8), (0, 4), (64, 2), (1, 64)])
        dtt_rep = self.apx(ps_dtT, [(8, 8), (2, 4), (1, 2), (0, 64)])
        xdt5 = fr["XDT"][:, :].rearrange("p (a r1 r0 k) -> p a r1 r0 k", a=8, r1=4, r0=2)
        self.op("vector", "tensor_tensor", xdt5, xt_rep, dtt_rep, ALU.mult)
        self.inc("vector", self.s_v, 1)
        v_xdt = self.s_v.n
        self.b_ps["D"].frees.append((self.s_v, self.s_v.n))
        if ch == 0:
            self.wait("vector", *self.v_prev0)
        if self.dbg_ssd and ch == 1:
            self._dbg_snap("dbg_prev", self.PREV[:, :])
            self._dbg_snap("dbg_xdt", fr["XDT"][:, :])
            self._dbg_snap("dbg_cstn", self.CSTN[:, 64:128])
            self._dbg_snap("dbg_dah", self.DAHL[:, 64:128])
            self._dbg_snap("dbg_dal",
                           self.DAHL[:, (self.NCH + 1) * 64:(self.NCH + 2) * 64])
            self._dbg_snap("dbg_dect", fr["DECT"][:, 0:64])
            self._dbg_snap("dbg_expc0", fr["EXPC0B"][:, 0:64])

        self.p.acquire("vector", self.b_fr["YST"])
        self._frst = {}
        self._fr_A(ch, c_loc, 0, v_dahl, v_cstn)
        self._fr_B(ch, c_loc, 0, v_xdt, s_decb)
        for g in range(1, 8):
            self._fr_A(ch, c_loc, g, v_dahl, v_cstn)
            self._fr_C(ch, c_loc, g - 1)
            self._fr_B(ch, c_loc, g, v_xdt, s_decb)
        self._fr_C(ch, c_loc, 7)

        self.wait("sync", self.s_v, self.s_v.n)
        yb = self.b_fr["YST"]
        self.dma("sync", self.d_yp[:, ch * 4096:(ch + 1) * 4096], fr["YST"][:, :], yb)
        yb.frees.append((yb.sem, yb.sem.n))

    def _fr_A(self, ch, c_loc, g, v_dahl, v_cstn):
        fr = self._fr
        h0 = 8 * g
        j0 = 4 * g
        col0 = c_loc * 128
        XBC = fr["XBC"]
        sl = g % 2
        self.p.acquire("tensor", self.b_ps["A"])
        self.wait("tensor", self.s_v, v_dahl)
        for i in range(8):
            h = h0 + i
            dah = self.DAHL[:, ch * 64 + h:ch * 64 + h + 1]
            dal = self.DAHL[:, (self.NCH + ch) * 64 + h:(self.NCH + ch) * 64 + h + 1]
            self.op("tensor", "matmul", self.PSA[:, i * 128:(i + 1) * 128],
                    dah.broadcast_to([128, 128]), self.ULEB[:, :],
                    start=(i == 0 or i == 4), stop=False)
            self.op("tensor", "matmul", self.PSA[:, i * 128:(i + 1) * 128],
                    dal.broadcast_to([128, 128]), self.ULEB[:, :],
                    start=False, stop=False)
            self.op("tensor", "matmul", self.PSA[:, i * 128:(i + 1) * 128],
                    self.IDB[:, :], self.MASKN[:, :], start=False, stop=(i == 3 or i == 7))
        self.inc("tensor", self.s_t, 1)
        t_mask = self.s_t.n
        e8 = fr["E8"][sl]
        self.p.acquire("scalar", self.b_e8[sl])
        self.wait("scalar", self.s_t, t_mask)
        self.wait("scalar", self.s_v, v_cstn)
        for i in range(8):
            h = h0 + i
            self.op("scalar", "activation", e8[:, i * 128:(i + 1) * 128],
                    self.PSA[:, i * 128:(i + 1) * 128], AF.Exp,
                    bias=self.CSTN[:, ch * 64 + h:ch * 64 + h + 1], scale=1.0)
        self.inc("scalar", self.s_s, 1)
        s_e8 = self.s_s.n
        self.b_ps["A"].frees.append((self.s_s, s_e8))
        if self.dbg_ssd and ch == 1 and g == 7:
            self._dbg_snap("dbg_e8", e8[:, :])
        self.p.acquire("tensor", self.b_pscg)
        for q in range(4):
            j = j0 + q
            self.op("tensor", "matmul", self.PSC[:, q * 128:(q + 1) * 128],
                    XBC[:, (8 + j // 4) * 512 + col0:(8 + j // 4) * 512 + col0 + 128],
                    XBC[:, (16 + j) * 512 + col0:(16 + j) * 512 + col0 + 128],
                    start=(q == 0), stop=(q == 3))
        self.inc("tensor", self.s_t, 1)
        self._frst[g] = dict(s_e8=s_e8, t_g=self.s_t.n)

    def _fr_B(self, ch, c_loc, g, v_xdt, s_decb):
        fr = self._fr
        st = self._frst[g]
        h0 = 8 * g
        j0 = 4 * g
        col0 = c_loc * 128
        XBC = fr["XBC"]
        sl = g % 2
        e8 = fr["E8"][sl]
        gl8 = fr["GL8"][sl]
        self.p.acquire("vector", self.b_gl8[sl])
        self.wait("vector", self.s_t, st["t_g"])
        self.wait("vector", self.s_s, st["s_e8"])
        psc_rep = self.apx(self.PSC[:, 0:512], [(128, 4), (0, 2), (1, 128)])
        self.op("vector", "tensor_tensor",
                gl8[:, :].rearrange("p (q h l) -> p q h l", q=4, h=2),
                e8[:, :].rearrange("p (q h l) -> p q h l", q=4, h=2),
                psc_rep, ALU.mult)
        self.inc("vector", self.s_v, 1)
        v_gl = self.s_v.n
        self.b_e8[sl].frees.append((self.s_v, v_gl))
        self.b_pscg.frees.append((self.s_v, v_gl))
        if self.dbg_ssd and ch == 1 and g == 7:
            self._dbg_snap("dbg_gl8", gl8[:, :])
        self.p.acquire("tensor", self.b_psbx)
        self.p.acquire("tensor", self.b_ps["D"])
        self.wait("tensor", self.s_s, st["s_e8"])
        ps_r = [self.PSB[:, 512:1024], self.PSD[:, 0:512]]
        for r in range(2):
            for q in range(4):
                i = 4 * r + q
                self.op("tensor", "matmul", ps_r[r][:, q * 128:(q + 1) * 128],
                        self.ONR[:, :], e8[0:1, i * 128:(i + 1) * 128],
                        start=True, stop=True)
        self.inc("tensor", self.s_t, 1)
        t_chx = self.s_t.n
        self.b_e8[sl].frees.append((self.s_t, t_chx))
        chx8 = fr["CHX8"][sl]
        self.p.acquire("vector", self.b_chx8[sl])
        self.wait("vector", self.s_t, t_chx)
        for r in range(2):
            c_base = XBC[:, (16 + j0 + 2 * r) * 512 + col0:(16 + j0 + 2 * r) * 512 + col0 + 128]
            c_rep = self.apx(c_base, [(512, 2), (0, 2), (1, 128)])
            self.op("vector", "tensor_tensor",
                    chx8[:, r * 512:(r + 1) * 512].rearrange(
                        "p (q h l) -> p q h l", q=2, h=2),
                    ps_r[r].rearrange("p (q h l) -> p q h l", q=2, h=2),
                    c_rep, ALU.mult)
        self.inc("vector", self.s_v, 1)
        v_chx = self.s_v.n
        self.b_psbx.frees.append((self.s_v, v_chx))
        self.b_ps["D"].frees.append((self.s_v, v_chx))
        if self.dbg_ssd and ch == 1 and g == 7:
            self._dbg_snap("dbg_chx", chx8[:, :])
        xdtd = fr["XDTD8"][sl]
        self.p.acquire("vector", self.b_xdtd[sl])
        self.wait("vector", self.s_v, v_xdt)
        self.wait("vector", self.s_s, s_decb)
        dect_rep = self.apx(fr["DECT"][:, h0:h0 + 8], [(1, 8), (0, 64)])
        self.op("vector", "tensor_tensor",
                xdtd[:, :].rearrange("p (h k) -> p h k", h=8),
                fr["XDT"][:, g * 512:(g + 1) * 512].rearrange("p (h k) -> p h k", h=8),
                dect_rep, ALU.mult)
        self.inc("vector", self.s_v, 1)
        v_xdtd = self.s_v.n
        prevbg = fr["PREVB"][:, sl * 512:sl * 512 + 512]
        self.p.acquire("vector", self.b_prevb[sl])
        exc0_rep = self.apx(fr["EXPC0B"][:, h0:h0 + 8], [(1, 8), (0, 64)])
        self.op("vector", "tensor_tensor",
                prevbg.rearrange("p (h k) -> p h k", h=8),
                self.PREV[:, g * 512:(g + 1) * 512].rearrange("p (h k) -> p h k", h=8),
                exc0_rep, ALU.mult)
        self.inc("vector", self.s_v, 1)
        st.update(v_gl=v_gl, v_chx=v_chx, v_xdtd=v_xdtd, v_prevbg=self.s_v.n)
        if self.dbg_ssd and ch == 1 and g == 7:
            self._dbg_snap("dbg_prevb", prevbg)

    def _fr_C(self, ch, c_loc, g):
        fr = self._fr
        st = self._frst[g]
        h0 = 8 * g
        j0 = 4 * g
        col0 = c_loc * 128
        XBC = fr["XBC"]
        sl = g % 2
        xdtd = fr["XDTD8"][sl]
        gl8 = fr["GL8"][sl]
        chx8 = fr["CHX8"][sl]
        prevbg = fr["PREVB"][:, sl * 512:sl * 512 + 512]
        self.p.acquire("tensor", self.b_pscs)
        self.wait("tensor", self.s_v, st["v_xdtd"])
        self.op("tensor", "matmul", self.PSC[:, 512:1024],
                fr["BT"][:, g * 128:(g + 1) * 128], xdtd[:, :],
                start=True, stop=True)
        self.inc("tensor", self.s_t, 1)
        t_s8 = self.s_t.n
        self.b_xdtd[sl].frees.append((self.s_t, t_s8))
        self.p.acquire("tensor", self.b_psby)
        self.wait("tensor", self.s_v, st["v_gl"])
        self.wait("tensor", self.s_v, st["v_chx"])
        self.wait("tensor", self.s_v, st["v_prevbg"])
        half_first = [True, True]
        for i in range(8):
            h = h0 + i
            q = i // 2
            odd = i % 2
            out = self.PSB[64 * odd:64 * odd + 64, q * 128:(q + 1) * 128]
            tp = (0, 64) if odd else None
            sk = bool(odd)
            self.op("tensor", "matmul", out,
                    fr["XDT"][:, h * 64:(h + 1) * 64], gl8[:, i * 128:(i + 1) * 128],
                    start=half_first[odd], stop=False, tile_position=tp,
                    skip_group_check=sk)
            half_first[odd] = False
            self.op("tensor", "matmul", out,
                    prevbg[:, i * 64:(i + 1) * 64], chx8[:, i * 128:(i + 1) * 128],
                    start=False, stop=(i >= 6), tile_position=tp,
                    skip_group_check=sk)
        self.inc("tensor", self.s_t, 1)
        t_y = self.s_t.n
        self.b_gl8[sl].frees.append((self.s_t, t_y))
        self.b_chx8[sl].frees.append((self.s_t, t_y))
        self.b_prevb[sl].frees.append((self.s_t, t_y))
        self.p.acquire("vector", self.b_fr["TSCR"])
        self.wait("vector", self.s_t, t_s8)
        cdb_rep = self.apx(fr["CDB"][:, h0:h0 + 8], [(1, 8), (0, 64)])
        self.op("vector", "tensor_tensor",
                fr["TSCR"][:, 0:512].rearrange("p (h k) -> p h k", h=8),
                self.PREV[:, g * 512:(g + 1) * 512].rearrange("p (h k) -> p h k", h=8),
                cdb_rep, ALU.mult)
        self.op("vector", "tensor_tensor", self.PREV[:, g * 512:(g + 1) * 512],
                fr["TSCR"][:, 0:512], self.PSC[:, 512:1024], ALU.add)
        self.inc("vector", self.s_v, 1)
        self.b_fr["TSCR"].frees.append((self.s_v, self.s_v.n))
        self.b_pscs.frees.append((self.s_v, self.s_v.n))
        self.p.acquire("vector", self.b_fr["DTRM"])
        self.wait("vector", self.s_t, t_y)
        xg_base = XBC[:, (j0 // 4) * 512 + col0:(j0 // 4) * 512 + col0 + 128]
        xg_rep = self.apx(xg_base, [(0, 4), (1, 128)])
        dpc_rep = self.apx(self.DPC[:, j0:j0 + 4], [(1, 4), (0, 128)])
        self.op("vector", "tensor_tensor",
                fr["DTRM"][:, 0:512].rearrange("p (q l) -> p q l", q=4),
                xg_rep, dpc_rep, ALU.mult)
        self.op("vector", "scalar_tensor_tensor", fr["YST"][:, g * 512:(g + 1) * 512],
                self.PSB[:, 0:512], 1.0, fr["DTRM"][:, 0:512], ALU.mult, ALU.add)
        self.inc("vector", self.s_v, 1)
        self.b_fr["DTRM"].frees.append((self.s_v, self.s_v.n))
        self.b_psby.frees.append((self.s_v, self.s_v.n))

    def _fr_psum_handoff(self):
        for src_b, dst in ((self.b_psby, "B"), (self.b_psbx, "B"),
                           (self.b_pscg, "C"), (self.b_pscs, "C")):
            self.b_ps[dst].frees.extend(src_b.frees)
            src_b.frees = []

    def _mamba_back(self):
        A = self.ARN
        bk = dict(
            CARRY=A[:, 0:8192].bitcast(F32),
            DPREV=A[:, 8192:12288],
            YPC=A[:, 12288:20480].bitcast(F32),
            CB=A[:, 20480:24576],
            SZB=A[:, 24576:28672],
            EXPB=[A[:, 50176:52224].bitcast(F32),
                  A[:, 52224:54272].bitcast(F32)],
            CHX=[A[:, 29696:30720], A[:, 61440:62464]],
            TF1=[A[:, 54272 + i * 1024:55296 + i * 1024].bitcast(F32) for i in range(3)],
            TF2=[A[:, 57344 + i * 1024:58368 + i * 1024].bitcast(F32) for i in range(3)],
            SQB=[A[:, 32768:33280], A[:, 60416:60928], A[:, 60928:61440]],
            YN=A[:, 33792:33792 + 4 * 4096],
        )
        self._bk = bk
        self.b_bk = {k: Buf(self.p) for k in ("CARRY", "DPREV", "YPC", "CB", "SZB",
                                              "YN")}
        self.b_expb = [Buf(self.p), Buf(self.p)]
        self.b_chxb = [Buf(self.p), Buf(self.p)]
        self.b_tfb = [Buf(self.p) for _ in range(3)]
        self.b_sqb = [Buf(self.p) for _ in range(3)]
        self.b_rowb = [Buf(self.p), Buf(self.p)]
        self.b_psb2 = [Buf(self.p), Buf(self.p)]
        self.b_psc1 = Buf(self.p)
        self.b_psc2 = Buf(self.p)
        for e in ENGINES:
            self.wait(e, self.s_t, self.s_t.n)
            self.wait(e, self.s_v, self.s_v.n)
            self.wait(e, self.s_s, self.s_s.n)
            self.wait(e, self.s_cc, self.s_cc.n)
            self.wait(e, self.s_spc, self.s_spc.n)
            for bb in (self.b_stg[0], self.b_stg[1], self.b_fr["YST"], self.b_iob[0],
                       self.b_iob[1], self.b_carry_out):
                if bb.sem is not None and bb.sem.n:
                    self.wait(e, bb.sem, bb.sem.n)

        cbuf = self.b_bk["CARRY"]
        self.dma("sync", bk["CARRY"][:, :], self.d_carry_rcv[0:128, :], cbuf)
        self.p.wait_buf("vector", cbuf)
        self.op("vector", "tensor_scalar", bk["CARRY"][:, :], bk["CARRY"][:, :],
                self.GAIN[:, 0:1], None, ALU.mult)
        self.inc("vector", self.s_v, 1)
        v_carry = self.s_v.n
        rows = self.ROWS
        CUM = rows[0:1, 0:self.NCH * 64]
        self.p.wait_buf("vector", self.b_clog)
        self.op("vector", "tensor_scalar", CUM[0:1, 0:64], self.CLOG[0:1, 0:64],
                0.0, None, ALU.mult)
        for c in range(1, self.NCH):
            self.op("vector", "tensor_tensor", CUM[0:1, c * 64:(c + 1) * 64],
                    CUM[0:1, (c - 1) * 64:c * 64],
                    self.CLOG[0:1, (c - 1) * 64:c * 64], ALU.subtract)
        self.inc("vector", self.s_v, 1)
        EXPD = rows[0:1, 1024:1024 + self.NCH * 64]
        self.wait("scalar", self.s_v, self.s_v.n)
        self.op("scalar", "activation", EXPD, CUM, AF.Exp, bias=0.0, scale=1.0)
        self.inc("scalar", self.s_s, 1)
        s_expd = self.s_s.n

        for bb in range(self.NMB):
            for c_loc in range(4):
                self._back_chunk(bb, c_loc, v_carry, s_expd)
            self._out_proj_block(bb)

    def _back_chunk(self, bb, c_loc, v_carry, s_expd):
        bk = self._bk
        ch = bb * 4 + c_loc
        for nm, src in (("YPC", self.d_yp), ("CB", self.d_c), ("SZB", self.d_sz)):
            b = self.b_bk[nm]
            self.p.acquire("sync", b)
            self.dma("sync", bk[nm][:, :], src[:, ch * 4096:(ch + 1) * 4096], b)
        self.p.acquire("tensor", self.b_ps["D"])
        self.wait("tensor", self.s_s, s_expd)
        ps_ed = self.PSD[:, 0:64]
        self.op("tensor", "matmul", ps_ed, self.ONR[:, :],
                self._bk_expd_row(ch), start=True, stop=True)
        self.inc("tensor", self.s_t, 1)
        bd = self.b_bk["DPREV"]
        self.p.acquire("vector", bd)
        self.wait("vector", self.s_t, self.s_t.n)
        self.wait("vector", self.s_v, v_carry)
        ed_rep = self.apx(ps_ed, [(1, 64), (0, 64)])
        self.op("vector", "tensor_tensor",
                bk["DPREV"][:, :].rearrange("p (h k) -> p h k", h=64),
                bk["CARRY"][:, :].rearrange("p (h k) -> p h k", h=64),
                ed_rep, ALU.mult)
        self.inc("vector", self.s_v, 1)
        v_dprev = self.s_v.n
        self.b_ps["D"].frees.append((self.s_v, self.s_v.n))

        self._bkst = {}
        for g in range(8):
            self._bk_A(ch, g)
            if g >= 2:
                self._bk_C(ch, g - 2)
            self._bk_B(ch, g, v_dprev)
            if g >= 2:
                self._bk_D(ch, g - 2)
        for g in (6, 7):
            self._bk_C(ch, g)
            self._bk_D(ch, g)

    def _bk_expd_row(self, ch):
        return self.ROWS[0:1, 1024 + ch * 64:1024 + (ch + 1) * 64]

    def _bk_A(self, ch, g):
        bk = self._bk
        h0 = 8 * g
        j0 = 4 * g
        self.p.acquire("tensor", self.b_ps["A"])
        for i in range(8):
            h = h0 + i
            dah = self.DAHL[:, ch * 64 + h:ch * 64 + h + 1]
            dal = self.DAHL[:, (self.NCH + ch) * 64 + h:(self.NCH + ch) * 64 + h + 1]
            self.op("tensor", "matmul", self.PSA[:, i * 128:(i + 1) * 128],
                    dah.broadcast_to([128, 128]), self.ULEB[:, :],
                    start=(i == 0 or i == 4), stop=False)
            self.op("tensor", "matmul", self.PSA[:, i * 128:(i + 1) * 128],
                    dal.broadcast_to([128, 128]), self.ULEB[:, :],
                    start=False, stop=(i == 3 or i == 7))
        self.inc("tensor", self.s_t, 1)
        t_rc = self.s_t.n
        sl2 = g % 2
        be = self.b_expb[sl2]
        expb = bk["EXPB"][sl2]
        self.p.acquire("scalar", be)
        self.wait("scalar", self.s_t, t_rc)
        self.op("scalar", "activation", expb[:, :], self.PSA[:, :], AF.Exp,
                bias=0.0, scale=1.0)
        self.inc("scalar", self.s_s, 1)
        s_eb = self.s_s.n
        self.b_ps["A"].frees.append((self.s_s, s_eb))
        bc = self.b_chxb[sl2]
        chx = bk["CHX"][sl2]
        self.p.acquire("vector", bc)
        self.wait("vector", self.s_s, s_eb)
        self.p.wait_buf("vector", self.b_bk["CB"])
        c_rep = self.apx(bk["CB"][:, j0 * 128:j0 * 128 + 128],
                         [(128, 4), (0, 2), (1, 128)])
        self.op("vector", "tensor_tensor",
                chx[:, :].rearrange("p (q h l) -> p q h l", q=4, h=2),
                expb[:, :].rearrange("p (q h l) -> p q h l", q=4, h=2),
                c_rep, ALU.mult)
        self.inc("vector", self.s_v, 1)
        be.frees.append((self.s_v, self.s_v.n))
        self._bkst[g] = dict(v_chx=self.s_v.n)

    def _bk_B(self, ch, g, v_dprev):
        bk = self._bk
        st = self._bkst[g]
        h0 = 8 * g
        sl2 = g % 2
        sl3 = g % 3
        chx = bk["CHX"][sl2]
        psb = self.PSB[:, sl2 * 512:(sl2 + 1) * 512]
        self.p.acquire("tensor", self.b_psb2[sl2])
        self.wait("tensor", self.s_v, st["v_chx"])
        self.wait("tensor", self.s_v, v_dprev)
        half_first = [True, True]
        for i in range(8):
            h = h0 + i
            q = i // 2
            odd = i % 2
            out = psb[64 * odd:64 * odd + 64, q * 128:(q + 1) * 128]
            tp = (0, 64) if odd else None
            self.op("tensor", "matmul", out,
                    bk["DPREV"][:, h * 64:(h + 1) * 64],
                    chx[:, i * 128:(i + 1) * 128],
                    start=half_first[odd], stop=(i >= 6), tile_position=tp,
                    skip_group_check=bool(odd))
            half_first[odd] = False
        self.inc("tensor", self.s_t, 1)
        t_dy = self.s_t.n
        self.b_chxb[sl2].frees.append((self.s_t, t_dy))
        self.b_bk["DPREV"].frees.append((self.s_t, t_dy))
        tf1 = bk["TF1"][sl3]
        tf2 = bk["TF2"][sl3]
        bt = self.b_tfb[sl3]
        self.p.acquire("vector", bt)
        self.wait("vector", self.s_t, t_dy)
        self.p.wait_buf("vector", self.b_bk["YPC"])
        self.p.wait_buf("vector", self.b_bk["SZB"])
        self.op("vector", "tensor_tensor", tf1[:, 0:512],
                bk["YPC"][:, g * 512:(g + 1) * 512], psb, ALU.add)
        self.op("vector", "tensor_tensor", tf2[:, 0:512],
                tf1[:, 0:512], bk["SZB"][:, g * 512:(g + 1) * 512], ALU.mult)
        self.inc("vector", self.s_v, 1)
        v_tf2 = self.s_v.n
        self.b_psb2[sl2].frees.append((self.s_v, v_tf2))
        sq = bk["SQB"][sl3]
        bs = self.b_sqb[sl3]
        self.p.acquire("scalar", bs)
        self.wait("scalar", self.s_v, v_tf2)
        self.op("scalar", "activation", sq[:, :], tf2[:, 0:512],
                AF.Square, bias=0.0, scale=1.0)
        self.inc("scalar", self.s_s, 1)
        st["s_sq"] = self.s_s.n

    def _bk_C(self, ch, g):
        bk = self._bk
        st = self._bkst[g]
        sl3 = g % 3
        self.p.acquire("tensor", self.b_psc1)
        self.wait("tensor", self.s_s, st["s_sq"])
        self.op("tensor", "matmul", self.PSC[0:1, 0:512], self.ONCB[:, :],
                bk["SQB"][sl3][:, :], start=True, stop=True)
        self.inc("tensor", self.s_t, 1)
        t_ms = self.s_t.n
        self.b_sqb[sl3].frees.append((self.s_t, t_ms))
        rstd = self.ROWS[0:1, 2048 + (g % 2) * 512:2560 + (g % 2) * 512]
        self.p.acquire("scalar", self.b_rowb[g % 2])
        self.wait("scalar", self.s_t, t_ms)
        self.op("scalar", "activation", rstd, self.PSC[0:1, 0:512], AF.Ln,
                bias=self.ACONST[0:1, 0:1], scale=self.ACONST[0:1, 2:3])
        self.schain()
        self.op("scalar", "activation", rstd, rstd, AF.Exp, bias=0.0,
                scale=self.ACONST[0:1, 1:2])
        self.inc("scalar", self.s_s, 1)
        st["s_rstd"] = self.s_s.n
        st["rstd"] = rstd
        self.b_psc1.frees.append((self.s_s, self.s_s.n))

    def _bk_D(self, ch, g):
        bk = self._bk
        st = self._bkst[g]
        j0 = 4 * g
        sl3 = g % 3
        self.p.acquire("tensor", self.b_psc2)
        self.wait("tensor", self.s_s, st["s_rstd"])
        self.op("tensor", "matmul", self.PSC[:, 512:1024], self.ONR[:, :], st["rstd"],
                start=True, stop=True)
        self.inc("tensor", self.s_t, 1)
        t_rb = self.s_t.n
        self.b_rowb[g % 2].frees.append((self.s_t, t_rb))
        self.wait("vector", self.s_t, t_rb)
        byn = self.b_bk["YN"]
        if ch % 4 == 0 and g == 0:
            self.p.acquire("vector", byn)
        tf2 = bk["TF2"][sl3]
        yslot = bk["YN"][:, (ch % 4) * 4096 + g * 512:(ch % 4) * 4096 + (g + 1) * 512]
        self.op("vector", "tensor_tensor", yslot,
                tf2[:, 0:512], self.PSC[:, 512:1024], ALU.mult)
        self.inc("vector", self.s_v, 1)
        v_yn = self.s_v.n
        self.b_psc2.frees.append((self.s_v, v_yn))
        self.b_tfb[sl3].frees.append((self.s_v, v_yn))
        self.b_bk["YPC"].frees.append((self.s_v, v_yn))
        self.b_bk["SZB"].frees.append((self.s_v, v_yn))
        self.b_bk["CB"].frees.append((self.s_v, v_yn))

    def _out_proj_block(self, bb):
        bk = self._bk
        byn = self.b_bk["YN"]
        v_yn = self.s_v.n
        if self._c_outp:
            for bs_ in self.b_wst:
                self.wait("gpsimd", bs_.sem, bs_.sem.n)
        for bwb in self.b_wr:
            self.p.acquire("gpsimd", bwb)
        for m in range(16):
            slot = m % 2
            wslot = m % 3
            wt = self.WR[:, wslot * 4096: wslot * 4096 + 4096]
            bw = self.b_wr[wslot]
            self.p.acquire("gpsimd", bw)
            v_sc = None
            if self._c_outp:
                self.dma("gpsimd", wt, self.d_wbout[m], bw)
            else:
                self.dma("gpsimd", wt,
                         self.din["w_outT"][m].rearrange("f c d -> f (c d)"), bw)
                self.p.wait_buf("vector", bw)
                nwrep = self.apx(self.NWPC[:, 0:32], [(1, 32), (0, 128)])
                self.op("vector", "tensor_tensor",
                        wt.rearrange("p (j d) -> p j d", j=32),
                        wt.rearrange("p (j d) -> p j d", j=32), nwrep, ALU.mult)
                self.inc("vector", self.s_v, 1)
                v_sc = self.s_v.n
                self.wait("sync", self.s_v, v_sc)
                self.dma("sync", self.d_wbout[m], wt, self.b_wst[wslot])
                bw.frees.append(self.b_wst[wslot].ready)
            ps = self.PSD[:, 0:512] if slot == 0 else self.PSD[:, 512:1024]
            self.p.acquire("tensor", self.b_ps["D"])
            self.p.wait_buf("tensor", bw)
            if v_sc is not None:
                self.wait("tensor", self.s_v, v_sc)
            if m == 0:
                self.wait("tensor", self.s_v, v_yn)
            for j in range(32):
                rhs = self.apx(bk["YN"][:, j * 128:j * 128 + 128], [(4096, 4), (1, 128)])
                self.op("tensor", "matmul", ps, wt[:, j * 128:(j + 1) * 128], rhs,
                        start=(j == 0), stop=(j == 31))
            self.inc("tensor", self.s_t, 1)
            bw.frees.append((self.s_t, self.s_t.n))
            b2 = self.b_iob[slot]
            self.p.acquire("sync", b2)
            xres = self.IOB[:, slot * 1024: slot * 1024 + 512]
            self.dma("sync", xres, self.d_r1[m * 128:(m + 1) * 128,
                                             bb * 512:(bb + 1) * 512], b2)
            self.p.wait_buf("vector", b2)
            self.wait("vector", self.s_t, self.s_t.n)
            self.op("vector", "tensor_tensor", xres, xres, ps, ALU.add)
            self.inc("vector", self.s_v, 1)
            b2.ready = (self.s_v, self.s_v.n)
            self.b_ps["D"].frees.append((self.s_v, self.s_v.n))
            self.p.wait_buf("sync", b2)
            self.dma("sync", self.d_r2[m * 128:(m + 1) * 128, bb * 512:(bb + 1) * 512],
                     xres, b2)
            b2.frees.append((b2.sem, b2.sem.n))
        byn.frees.append((self.s_t, self.s_t.n))
        self._c_outp = True



def kernel(**inputs):
    cores = list(range(8))
    T = 2048
    in_maps = stage_inputs(inputs, cores, T)
    k = Kern(T=T, pair_groups=[[0, 1], [2, 3], [4, 5], [6, 7]])
    res = run_bass_kernel_spmd(k.nc, in_maps, core_ids=cores)
    u = np.asarray(inputs["L0_attn_u"])
    out = np.empty_like(u, dtype=np.float32)
    for i, c in enumerate(cores):
        s, h = c // 2, c % 2
        out[s, h * T:(h + 1) * T, :] = res.results[i]["out_fm"].T
    return out


# version 11
# speedup vs baseline: 1.0442x; 1.0442x over previous
import numpy as np
import ml_dtypes
import concourse.bass as bass
import concourse.mybir as mybir
from concourse.bass_utils import run_bass_kernel_spmd

F32 = mybir.dt.float32
BF16 = mybir.dt.bfloat16
F8 = mybir.dt.float8e4
AF = mybir.ActivationFunctionType
ALU = mybir.AluOpType
BF = ml_dtypes.bfloat16

D = 2048
DI = 4096
DXB = 1024
DST = 128
HD = 64
NH = 64
NPAIR = 32
DCONV = 4
CONVD = DI + 2 * DXB
DIP = DI + CONVD + NH
IFF = 6144
CK = 128
EPS = 1e-5

ENGINES = ("sync", "vector", "scalar", "tensor", "gpsimd")


class Sem:
    __slots__ = ("name", "n", "handle")

    def __init__(self, name):
        self.name = name
        self.n = 0
        self.handle = None


class Op:
    __slots__ = ("fname", "args", "kw", "waits", "incs")

    def __init__(self, fname, args, kw):
        self.fname = fname
        self.args = args
        self.kw = kw
        self.waits = []
        self.incs = []


class Buf:
    __slots__ = ("ready", "frees", "_sem", "_prog")
    _n = 0

    def __init__(self, prog=None):
        self.ready = None
        self.frees = []
        self._prog = prog
        self._sem = None

    @property
    def sem(self):
        if self._sem is None and self._prog is not None:
            Buf._n += 1
            self._sem = self._prog.sem(f"b{Buf._n}")
        return self._sem


def _ap_ranges(a):
    try:
        t = a.tensor
        esz = mybir.dt.size(a.dtype)
        off = a.offset * esz
        span = esz
        for st, ct in list(a.ap)[1:]:
            span += abs(st) * (ct - 1) * esz
        p0 = t.base_partition + (list(a.ap)[0][0] and 0)
        pc = list(a.ap)[0][1]
        return (t.name, off, off + span, p0, p0 + pc)
    except Exception:
        return None


def _overlap(r1, r2):
    if r1 is None or r2 is None:
        return True
    if r1[0] != r2[0]:
        return False
    return r1[1] < r2[2] and r2[1] < r1[2]


_OUT_KW = ("out", "dest")


class Prog:
    DRAIN_ENGS = ("vector", "scalar")

    def __init__(self):
        self.q = {e: [] for e in ENGINES}
        self.sems = {}
        self._written = {e: [] for e in ENGINES}

    def sem(self, name):
        if name not in self.sems:
            self.sems[name] = Sem(name)
        return self.sems[name]

    @staticmethod
    def _split_io(fname, args, kw):
        outs, ins = [], []
        aps = [a for a in args if hasattr(a, "tensor")]
        for k, v in kw.items():
            if hasattr(v, "tensor"):
                (outs if k in _OUT_KW else ins).append(v)
        if aps:
            outs.append(aps[0])
            ins.extend(aps[1:])
        return outs, ins

    def op(self, eng, fname, *args, **kw):
        if eng in self.DRAIN_ENGS and fname not in (None, "drain", "nop"):
            outs, ins = self._split_io(fname, args, kw)
            rds = [_ap_ranges(a) for a in ins + outs]
            hazard = any(_overlap(w, r) for w in self._written[eng] for r in rds)
            if hazard:
                self.q[eng].append(Op("drain", (), {}))
                self._written[eng] = []
            self._written[eng].extend(_ap_ranges(a) for a in outs)
        o = Op(fname, args, kw)
        self.q[eng].append(o)
        return o

    def wait(self, eng, sem, val=None):
        if val is None:
            val = sem.n
        if val <= 0:
            return
        lst = self.q[eng]
        o = Op(None, (), {})
        o.waits.append((sem, val))
        lst.append(o)

    def wait_buf(self, eng, buf):
        if buf.ready is None:
            return
        if isinstance(buf.ready, list):
            for s, v in buf.ready:
                self.wait(eng, s, v)
        else:
            self.wait(eng, *buf.ready)

    def free_buf(self, eng_unused, buf, sem, val=None):
        buf.frees.append((sem, sem.n if val is None else val))

    @staticmethod
    def _merge(frees):
        m = {}
        for s, v in frees:
            if id(s) not in m or m[id(s)][1] < v:
                m[id(s)] = (s, v)
        return list(m.values())

    def acquire(self, eng, buf):
        for s, v in self._merge(buf.frees):
            self.wait(eng, s, v)
        buf.frees = []
        buf.ready = None

    def acquire2(self, engs, buf):
        for e in engs:
            for s, v in self._merge(buf.frees):
                self.wait(e, s, v)
        buf.frees = []
        buf.ready = None

    def inc(self, eng, sem, n=1):
        lst = self.q[eng]
        assert lst and lst[-1].fname is not None, f"inc on {eng} with no op"
        lst[-1].incs.append((sem, n))
        sem.n += n

    def check(self):
        pcs = {e: 0 for e in ENGINES}
        vals = {id(s): 0 for s in self.sems.values()}
        total = sum(len(self.q[e]) for e in ENGINES)
        done = 0
        progress = True
        while progress:
            progress = False
            for e in ENGINES:
                while pcs[e] < len(self.q[e]):
                    o = self.q[e][pcs[e]]
                    if any(vals[id(s)] < v for s, v in o.waits):
                        break
                    for s, n in o.incs:
                        vals[id(s)] += n
                    pcs[e] += 1
                    done += 1
                    progress = True
        if done != total:
            msg = []
            for e in ENGINES:
                if pcs[e] < len(self.q[e]):
                    o = self.q[e][pcs[e]]
                    blocked = [(s.name, v, vals[id(s)]) for s, v in o.waits if vals[id(s)] < v]
                    msg.append(f"{e} stuck at {pcs[e]}/{len(self.q[e])} op={o.fname} waiting {blocked}")
            raise RuntimeError("DEADLOCK in planned program:\n" + "\n".join(msg))

    def emit(self, block):
        def runner(ops):
            def f(engine_obj):
                for o in ops:
                    for s, v in o.waits:
                        engine_obj.wait_ge(s.handle, v)
                    if o.fname is None:
                        continue
                    inst = getattr(engine_obj, o.fname)(*o.args, **o.kw)
                    for s, n in o.incs:
                        inst = inst.then_inc(s.handle, n)
            return f
        block.sync(runner(self.q["sync"]))
        block.vector(runner(self.q["vector"]))
        block.scalar(runner(self.q["scalar"]))
        block.tensor(runner(self.q["tensor"]))
        block.gpsimd(runner(self.q["gpsimd"]))


def stage_inputs(inp, cores, T):
    u = np.asarray(inp["L0_attn_u"])
    W_in = np.asarray(inp["L0_attn_W_in"])
    conv_w = np.asarray(inp["L0_attn_conv_w"])
    conv_b = np.asarray(inp["L0_attn_conv_b"])
    dt_bias = np.asarray(inp["L0_attn_dt_bias"])
    A_log = np.asarray(inp["L0_attn_A_log"])
    Dv = np.asarray(inp["L0_attn_D"])
    nw = np.asarray(inp["L0_attn_norm_weight"])
    W_out = np.asarray(inp["L0_attn_W_out"])
    gw = np.asarray(inp["L0_ffn_gate_w"])
    uw = np.asarray(inp["L0_ffn_up_w"])
    dw = np.asarray(inp["L0_ffn_down_w"])
    gs = np.asarray(inp["L0_ffn_gate_scale"])
    us = np.asarray(inp["L0_ffn_up_scale"])
    ds = np.asarray(inp["L0_ffn_down_scale"])

    w_gate = np.ascontiguousarray(gw.reshape(16, 128, 48, 128).transpose(2, 1, 0, 3))
    w_up = np.ascontiguousarray(uw.reshape(16, 128, 48, 128).transpose(2, 1, 0, 3))
    w_down = np.ascontiguousarray(dw.reshape(48, 128, 16, 128).transpose(2, 1, 0, 3))
    WiT = np.ascontiguousarray(W_in.T)
    w_inT = np.ascontiguousarray(WiT[:, :10240].reshape(16, 128, 80, 128).transpose(2, 1, 0, 3))
    w_indt = np.ascontiguousarray(WiT[:, 10240:].reshape(16, 128, 64).transpose(1, 0, 2))
    WoT = np.ascontiguousarray(W_out.T)
    w_outT = np.ascontiguousarray(WoT.reshape(32, 128, 16, 128).transpose(2, 1, 0, 3))

    v_convw = np.ascontiguousarray(conv_w.reshape(48, 128, 4).transpose(1, 0, 2).reshape(128, 192))
    v_convb = np.ascontiguousarray(conv_b.reshape(48, 128).T)
    v_dtb = np.ascontiguousarray(dt_bias.reshape(64, 1))
    v_alog = np.ascontiguousarray(A_log.reshape(64, 1))
    v_dpc = np.ascontiguousarray(
        np.where(np.arange(128)[:, None] < 64, Dv[0::2][None, :], Dv[1::2][None, :]))
    v_nwpc = np.ascontiguousarray(nw.reshape(32, 128).T)
    v_gs = np.ascontiguousarray(gs.reshape(48, 128).T)
    v_us = np.ascontiguousarray(us.reshape(48, 128).T)
    v_ds = np.ascontiguousarray(ds.reshape(16, 128).T)

    c_actconst = np.tile(np.array([[1e-5, -0.5, 1.0 / 128.0, 0.5, -1.0, 2.0, 0.0, 1.0]],
                                   np.float32), (128, 1))
    c_ident_bf = np.eye(128, dtype=BF)
    c_ident_f32 = np.eye(128, dtype=np.float32)
    c_ones_row = np.ones((1, 128), np.float32)
    c_ones_col_bf = np.ones((128, 1), BF)
    idx = np.arange(128)
    c_ule = (idx[:, None] <= idx[None, :]).astype(np.float32)
    c_slt = (idx[:, None] > idx[None, :]).astype(np.float32)
    c_t01 = (idx[None, :] >= idx[:, None]).astype(np.float32)
    c_uleb = c_ule.astype(BF)
    c_sltb = c_slt.astype(BF)
    c_maskneg = (-1.0e9 * (idx[None, :] < idx[:, None])).astype(BF)
    c_ones128b = np.ones((128, 128), BF)
    c_e0b = np.zeros((128, 128), BF)
    c_e0b[0, :] = 1
    c_onesrowb = np.ones((1, 128), BF)
    c_ones128f = np.ones((128, 128), np.float32)

    c_zeros = np.zeros((128, 1024), np.float32)
    shared = dict(
        w_gate=w_gate, w_up=w_up, w_down=w_down, w_inT=w_inT, w_indt=w_indt,
        w_outT=w_outT, v_convw=v_convw, v_convb=v_convb, v_dtb=v_dtb,
        v_alog=v_alog, v_dpc=v_dpc.astype(np.float32), v_nwpc=v_nwpc,
        v_gs=v_gs, v_us=v_us, v_ds=v_ds,
        c_actconst=c_actconst,
        c_ident_bf=c_ident_bf, c_ident_f32=c_ident_f32, c_ones_row=c_ones_row,
        c_ones_col_bf=c_ones_col_bf, c_t01=c_t01,
        c_zeros=c_zeros, c_uleb=c_uleb, c_sltb=c_sltb, c_maskneg=c_maskneg,
        c_ones128b=c_ones128b, c_e0b=c_e0b,
    )
    in_maps = []
    for k in cores:
        s, h = k // 2, k % 2
        m = dict(shared)
        m["u_fm"] = np.ascontiguousarray(u[s, h * T:(h + 1) * T, :].T)
        m["gain"] = np.full((128, 1), 1.0 if h == 1 else 0.0, np.float32)
        in_maps.append(m)
    return in_maps


class Kern:
    def __init__(self, T=2048, pair_groups=None, stop_after=None, dbg_ssd=False):
        assert T % 512 == 0
        self.T = T
        self.NMB = T // 512
        self.NCH = T // CK
        self.FFN_BLKS = [1024] * (T // 1024) if T % 1024 == 0 else [512] * (T // 512)
        self.pair_groups = pair_groups or [[0, 1], [2, 3], [4, 5], [6, 7]]
        self.stop_after = stop_after
        self.dbg_ssd = dbg_ssd
        self.p = Prog()
        self.nc = bass.Bass("TRN2", target_bir_lowering=False, debug=False)
        self._build()

    def _alloc(self):
        nc = self.nc
        self._ctx = []

        def sb(name, shape, dt=F32):
            g = nc.sbuf_tensor(name, shape, dt)
            t = g.__enter__()
            self._ctx.append(g)
            return t

        T = self.T
        din = {}

        def dr_in(name, shape, dt=F32):
            din[name] = nc.dram_tensor(name, list(shape), dt, kind="ExternalInput").ap()

        dr_in("u_fm", (D, T))
        dr_in("w_gate", (48, 128, 16, 128)); dr_in("w_up", (48, 128, 16, 128))
        dr_in("w_down", (16, 128, 48, 128))
        dr_in("w_inT", (80, 128, 16, 128)); dr_in("w_indt", (128, 16, 64))
        dr_in("w_outT", (16, 128, 32, 128))
        dr_in("v_convw", (128, 48 * 4)); dr_in("v_convb", (128, 48))
        dr_in("v_dtb", (64, 1)); dr_in("v_alog", (64, 1))
        dr_in("v_dpc", (128, 32)); dr_in("v_nwpc", (128, 32))
        dr_in("v_gs", (128, 48)); dr_in("v_us", (128, 48)); dr_in("v_ds", (128, 16))
        dr_in("c_actconst", (128, 8))
        dr_in("c_ident_bf", (128, 128), BF16); dr_in("c_ident_f32", (128, 128))
        dr_in("c_ones_row", (1, 128)); dr_in("c_ones_col_bf", (128, 1), BF16)
        dr_in("c_t01", (128, 128))
        dr_in("c_zeros", (128, 1024))
        dr_in("c_uleb", (128, 128), BF16); dr_in("c_sltb", (128, 128), BF16)
        dr_in("c_e0b", (128, 128), BF16)
        dr_in("c_maskneg", (128, 128), BF16); dr_in("c_ones128b", (128, 128), BF16)
        dr_in("gain", (128, 1))
        self.din = din

        self.d_out = nc.dram_tensor("out_fm", [D, T], F32, kind="ExternalOutput").ap()

        def dr_s(name, shape, dt=F32):
            return nc.dram_tensor(name, list(shape), dt, kind="Internal").ap()

        self.d_r1 = dr_s("sp_r1", (D, T))
        self.d_r2 = dr_s("sp_r2", (D, T))
        self.d_w8gu = dr_s("wc8_gu", (48, 128, 4096), F8)
        self.d_w8d = dr_s("wc8_down", (16, 128, 6144), F8)
        self.d_wbin = dr_s("wcb_in", (81, 128, 2048), BF16)
        self.d_wbout = dr_s("wcb_out", (16, 128, 4096), BF16)
        nmb = self.NMB
        self.d_sz = dr_s("sp_sz", (128, nmb * NPAIR * 512), BF16)
        self.d_c = dr_s("sp_c", (128, nmb * NPAIR * 512), BF16)
        self.d_yp = dr_s("sp_yp", (128, nmb * NPAIR * 512))
        self.d_halo_in = dr_s("cc_halo_in", (D, 4))
        self.d_halo_out = dr_s("cc_halo_out", (2 * D, 4))
        self.d_carry_snd = dr_s("cc_carry_snd", (128, 4096))
        self.d_carry_rcv = dr_s("cc_carry_rcv", (2 * 128, 4096))

        self.ARN = sb("arena", [128, 65536], BF16)

        DTCS = sb("dtcs", [128, T])
        self.DTF = DTCS[0:64, :]
        self.CSF = DTCS[64:128, :]
        self.CLOG = sb("clog", [1, self.NCH * 64])
        self.PREV = self.ARN[:, 49152 + 16 * 512:49152 + 16 * 512 + 8192].bitcast(F32)
        self.CARRY = None
        self.WR = sb("wring", [128, 2 * 6144], BF16)
        self.IOB = sb("iob", [128, 2 * 1024])
        self.STG = sb("stg", [128, 2 * 2048], BF16)
        self.ROWS = sb("rows", [1, 3 * 1024])
        self.HALO = sb("halo", [128, 16 * 3])
        self.XBCH = sb("xbch", [128, 48 * 3])
        self.TAIL = sb("tail", [128, 48 * 3])

        self.ACONST = sb("aconst", [128, 8])
        self.IDB = sb("idb", [128, 128], BF16)
        self.IDF = sb("idf", [128, 128])
        self.ONR = sb("onr", [1, 128])
        self.ONCB = sb("oncb", [128, 1], BF16)
        self.T01 = sb("t01", [128, 128])
        self.ULEB = sb("uleb", [128, 128], BF16)
        self.SLTB = sb("sltb", [128, 128], BF16)
        self.E0B = sb("e0b", [128, 128], BF16)
        self.MASKN = sb("maskn", [128, 128], BF16)
        self.ONES128B = sb("ones128b", [128, 128], BF16)
        self.DAHL = sb("dahl", [128, 2 * self.NCH * 64], BF16)
        self.CSTN = sb("cstn", [128, max(self.NCH * 64, 1024)])
        self.F8V = self.CSTN[:, 0:1024].bitcast(F8)
        self.CONVW = sb("convw", [128, 48 * 4])
        self.CONVB = sb("convb", [128, 48])
        self.DTBias = sb("dtb", [64, 1])
        self.ALOG = sb("alog", [64, 1])
        self.ACOL = sb("acol", [64, 1])
        self.DPC = sb("dpc", [128, 32])
        self.NWPC = sb("nwpc", [128, 32])
        self.GSC = sb("gsc", [128, 48])
        self.USC = sb("usc", [128, 48])
        self.DSC = sb("dsc", [128, 16])
        self.DSH = sb("dsh", [128, 16])
        self.GAIN = sb("gainp", [128, 1])

        self.PSA = self._psalloc("psA")
        self.PSB = self._psalloc("psB")
        self.PSC = self._psalloc("psC")
        self.PSD = self._psalloc("psD")

    def _psalloc(self, name):
        g = self.nc.psum_tensor(name, [128, 1024], F32)
        t = g.__enter__()
        self._ctx.append(g)
        return t

    def op(self, *a, **k): return self.p.op(*a, **k)
    def wait(self, *a, **k): return self.p.wait(*a, **k)
    def inc(self, *a, **k): return self.p.inc(*a, **k)

    def dma(self, eng, out, in_, sem, **kw):
        self.op(eng, "dma_start", out=out, in_=in_, **kw)
        if isinstance(sem, Buf):
            self.inc(eng, sem.sem, 16)
            sem.ready = (sem.sem, sem.sem.n)
        else:
            self.inc(eng, sem, 16)

    def wcast(self, cache, in_, parity):
        b = self.b_c8[parity]
        self.op("gpsimd", "dma_start", out=cache, in_=in_)
        self.inc("gpsimd", b.sem, 16)
        return (b.sem, b.sem.n)

    def dma_q8(self, out, in_, buf, cache=None):
        n = out.shape[1]
        if isinstance(buf.ready, list):
            ready = buf.ready
        elif buf.ready is not None:
            ready = [buf.ready]
        else:
            ready = []
        for off in range(0, n, 2048):
            m = min(2048, n - off)
            sl = self._f8i % 2
            self._f8i += 1
            s = self.s_f8s[sl]
            f8 = self.F8V[:, sl * 2048: sl * 2048 + m]
            self.wait("gpsimd", s, s.n)
            self.op("gpsimd", "dma_start", out=f8, in_=in_[:, off:off + m])
            self.inc("gpsimd", s, 16)
            self.wait("gpsimd", s, s.n)
            if cache is not None:
                self.op("gpsimd", "dma_start", out=cache[:, off:off + m], in_=f8)
                self.inc("gpsimd", s, 16)
                self.wait("gpsimd", s, s.n)
            self.op("gpsimd", "dma_start", out=out[:, off:off + m], in_=f8)
            self.inc("gpsimd", s, 16)
            ready.append((s, s.n))
        buf.ready = ready

    def schain(self):
        self.inc("scalar", self.s_s, 1)
        self.wait("scalar", self.s_s, self.s_s.n)

    def barrier(self, tag):
        p = self.p
        s = p.sem(f"bar_{tag}")
        for e in ENGINES:
            self.op(e, "nop")
            self.inc(e, s, 1)
        for e in ENGINES:
            self.wait(e, s, 5 if False else len(ENGINES))

    def _build(self):
        self._alloc()
        p = self.p
        self.s_ld = p.sem("s_ld")
        self.s_w = p.sem("s_w")
        self.s_x = p.sem("s_x")
        self.s_sp = p.sem("s_sp")
        self.s_t = p.sem("s_t")
        self.s_v = p.sem("s_v")
        self.s_s = p.sem("s_s")
        self.s_cc = p.sem("s_cc")
        self.s_out = p.sem("s_out")

        self.b_iob = [Buf(self.p), Buf(self.p)]
        self.b_stg = [Buf(self.p), Buf(self.p)]
        self.b_wr = [Buf(self.p) for _ in range(6)]
        self.b_wst = [Buf(self.p) for _ in range(6)]

        self._c_ffn = False
        self.b_c8 = [Buf(self.p), Buf(self.p)]
        self._castv = {}
        self._c_inp = False
        self._c_outp = False
        self.b_ps = {"A": Buf(), "B": Buf(), "C": Buf(), "D": Buf()}
        self.s_spc = self.p.sem("s_spc")
        self.s_f8 = self.p.sem("s_f8")
        self.s_f8s = [self.p.sem("s_f8a"), self.p.sem("s_f8b")]
        self._f8i = 0
        self.b_halo = Buf(self.p)
        self.b_clog = Buf(self.p)
        self.s_spz = self.p.sem("s_spz")

        self._load_consts()
        self._ffn(x_hbm=self.din["u_fm"], dst_hbm=self.d_r1, collect_halo=True)
        if self.stop_after == "ffn1":
            self._finalize_dev(copy_from=self.d_r1)
            self._emit()
            return
        self._mamba_front()
        if self.stop_after == "front":
            self._dev_dump([("dbg_yp", self.d_yp, [128, self.NCH * 4096]),
                            ("dbg_sz", self.d_sz, [128, self.NMB * NPAIR * 512]),
                            ("dbg_carry", self.d_carry_rcv, [256, 4096]),
                            ("dbg_dr1", self.d_r1, [D, self.T]),
                            ("dbg_c", self.d_c, [128, self.NMB * NPAIR * 512])])
            self._emit()
            return
        self._mamba_back()
        if self.stop_after == "back":
            self._finalize_dev(copy_from=self.d_r2)
            self._emit()
            return
        self._ffn(x_hbm=self.d_r2, dst_hbm=self.d_out, collect_halo=False)
        self.wait("sync", self.b_iob[0].sem, self.b_iob[0].sem.n)
        self.wait("sync", self.b_iob[1].sem, self.b_iob[1].sem.n)
        self._emit()

    def _emit(self):
        self.p.check()
        nc = self.nc
        semctx = []
        for s in self.p.sems.values():
            g = nc.semaphore(s.name)
            s.handle = g.__enter__()
            semctx.append(g)
        blk = nc.Block()
        b = blk.__enter__()
        self.p.emit(b)
        blk.__exit__(None, None, None)
        for g in semctx:
            g.__exit__(None, None, None)
        for g in reversed(self._ctx):
            g.__exit__(None, None, None)

    def _dbg_snap(self, name, src):
        dst = self.nc.dram_tensor(name, list(src.shape), src.dtype,
                                  kind="ExternalOutput").ap()
        b = Buf(self.p)
        self.wait("sync", self.s_v, self.s_v.n)
        self.wait("sync", self.s_t, self.s_t.n)
        self.wait("sync", self.s_s, self.s_s.n)
        self.dma("sync", dst, src, b)
        for e in ("vector", "scalar", "tensor"):
            self.p.wait_buf(e, b)

    def _dev_dump(self, specs):
        for s in list(self.p.sems.values()):
            if s.n > 0:
                self.wait("sync", s, s.n)
        for name, src_ap, shape in specs:
            dst = self.nc.dram_tensor(name, list(shape), src_ap.dtype,
                                      kind="ExternalOutput").ap()
            nrow, ncol = shape
            for r0 in range(0, nrow, 128):
                rows = min(128, nrow - r0)
                for c0 in range(0, ncol, 2048):
                    cols = min(2048, ncol - c0)
                    if src_ap.dtype == BF16:
                        buf = self.STG[0:rows, 0:cols]
                    else:
                        buf = self.IOB[0:rows, 0:cols]
                    self.dma("sync", buf, src_ap[r0:r0 + rows, c0:c0 + cols], self.s_x)
                    self.wait("sync", self.s_x, self.s_x.n)
                    self.dma("sync", dst[r0:r0 + rows, c0:c0 + cols], buf, self.s_out)
                    self.wait("sync", self.s_out, self.s_out.n)

    def _finalize_dev(self, copy_from):
        for k in range(16):
            slot = k % 2
            buf = self.IOB[:, slot * 1024:(slot + 1) * 1024]
            b = self.b_iob[slot]
            nh = self.T // 1024 if self.T >= 1024 else 1
            w = min(1024, self.T)
            for h in range(nh):
                self.p.acquire("sync", b)
                self.dma("sync", buf[:, 0:w],
                         copy_from[k * 128:(k + 1) * 128, h * w:(h + 1) * w], b)
                self.p.wait_buf("sync", b)
                self.dma("sync", self.d_out[k * 128:(k + 1) * 128, h * w:(h + 1) * w],
                         buf[:, 0:w], b)
                b.frees.append((b.sem, b.sem.n))
        self.wait("sync", self.s_out, self.s_out.n)

    def _load_consts(self):
        d = self.din
        ld = self.s_ld
        for dst, src in [
            (self.ACONST, "c_actconst"),
            (self.IDB, "c_ident_bf"), (self.IDF, "c_ident_f32"),
            (self.ONR, "c_ones_row"), (self.ONCB, "c_ones_col_bf"),
            (self.T01, "c_t01"),
            (self.ULEB, "c_uleb"), (self.SLTB, "c_sltb"), (self.E0B, "c_e0b"),
            (self.MASKN, "c_maskneg"),
            (self.ONES128B, "c_ones128b"),
            (self.CONVW, "v_convw"), (self.CONVB, "v_convb"),
            (self.DTBias, "v_dtb"), (self.ALOG, "v_alog"),
            (self.DPC, "v_dpc"), (self.NWPC, "v_nwpc"),
            (self.GSC, "v_gs"), (self.USC, "v_us"), (self.DSC, "v_ds"),
            (self.GAIN, "gain"),
        ]:
            self.dma("sync", dst[:, :], d[src][:, :], ld)
        self.wait("scalar", ld, ld.n)
        self.op("scalar", "activation", self.ACOL[:, :], self.ALOG[:, :], AF.Exp,
                bias=0.0, scale=1.0)
        self.inc("scalar", self.s_s, 1)
        self.wait("vector", ld, ld.n)
        self.wait("vector", self.s_s, self.s_s.n)
        self.op("vector", "tensor_scalar", self.ACOL[:, :], self.ACOL[:, :], -1.0, None, ALU.mult)
        self.op("vector", "tensor_scalar", self.DSH[:, :], self.DSC[:, :], 0.5, None, ALU.mult)
        self.inc("vector", self.s_v, 1)
        for e in ("tensor", "gpsimd", "sync"):
            self.wait(e, ld, ld.n)
            self.wait(e, self.s_v, self.s_v.n)

    def _layernorm_to_xn(self, x_hbm, tok0, TB, XN):
        nck = TB // 512
        for nm in ("A", "B"):
            self.p.acquire("tensor", self.b_ps[nm])
        for k in range(16):
            slot = k % 2
            xin = self.IOB[:, slot * 1024: slot * 1024 + TB]
            b = self.b_iob[slot]
            self.p.acquire("sync", b)
            self.dma("sync", xin, x_hbm[k * 128:(k + 1) * 128, tok0:tok0 + TB], b)
            bs = self.b_stg[slot]
            self.p.acquire2(("vector", "scalar"), bs)
            self.p.wait_buf("vector", b)
            xb = self.STG[:, slot * 2048: slot * 2048 + TB]
            self.op("vector", "tensor_copy", xb, xin)
            self.inc("vector", self.s_v, 1)
            sq = self.STG[:, slot * 2048 + 1024: slot * 2048 + 1024 + TB]
            self.p.wait_buf("scalar", b)
            self.op("scalar", "activation", sq, xin, AF.Square, bias=0.0, scale=1.0)
            self.inc("scalar", self.s_s, 1)
            self.p.free_buf(None, b, self.s_v)
            self.p.free_buf(None, b, self.s_s)
            self.wait("tensor", self.s_v, self.s_v.n)
            self.wait("tensor", self.s_s, self.s_s.n)
            for h in range(nck):
                self.op("tensor", "matmul", self.PSA[0:1, h * 512:(h + 1) * 512],
                        self.ONCB[:, :], xb[:, h * 512:(h + 1) * 512],
                        start=(k == 0), stop=(k == 15))
                self.op("tensor", "matmul", self.PSB[0:1, h * 512:(h + 1) * 512],
                        self.ONCB[:, :], sq[:, h * 512:(h + 1) * 512],
                        start=(k == 0), stop=(k == 15))
            self.inc("tensor", self.s_t, 1)
            self.p.free_buf(None, bs, self.s_t)
        rows = self.ROWS
        mu = rows[0:1, 0:TB]
        msq = rows[0:1, 1024:1024 + TB]
        var = msq
        rstd = rows[0:1, 2048:2048 + TB]
        self.wait("vector", self.s_t, self.s_t.n)
        self.op("vector", "tensor_scalar", mu, self.PSA[0:1, 0:TB], 1.0 / D, None, ALU.mult)
        self.inc("vector", self.s_v, 1)
        self.wait("scalar", self.s_v, self.s_v.n)
        self.op("scalar", "activation", msq, mu, AF.Square, bias=0.0, scale=1.0)
        self.inc("scalar", self.s_s, 1)
        self.wait("vector", self.s_s, self.s_s.n)
        self.op("vector", "scalar_tensor_tensor", var, self.PSB[0:1, 0:TB], 1.0 / D, msq,
                ALU.mult, ALU.subtract)
        self.inc("vector", self.s_v, 1)
        self.b_ps["A"].frees.append((self.s_v, self.s_v.n))
        self.b_ps["B"].frees.append((self.s_v, self.s_v.n))
        self.wait("scalar", self.s_v, self.s_v.n)
        self.op("scalar", "activation", var, var, AF.Ln, bias=self.ACONST[0:1, 0:1], scale=1.0)
        self.schain()
        self.op("scalar", "activation", rstd, var, AF.Exp, bias=0.0,
                scale=self.ACONST[0:1, 1:2])
        self.inc("scalar", self.s_s, 1)
        for nm in ("C", "D"):
            self.p.acquire("tensor", self.b_ps[nm])
        self.wait("tensor", self.s_s, self.s_s.n)
        for h in range(nck):
            self.op("tensor", "matmul", self.PSC[:, h * 512:(h + 1) * 512],
                    self.ONR[:, :], mu[0:1, h * 512:(h + 1) * 512], start=True, stop=True)
            self.op("tensor", "matmul", self.PSD[:, h * 512:(h + 1) * 512],
                    self.ONR[:, :], rstd[0:1, h * 512:(h + 1) * 512], start=True, stop=True)
        self.inc("tensor", self.s_t, 1)
        t_bc = self.s_t.n
        self._v_xn_chunk = [0] * 16
        for k in range(16):
            slot = k % 2
            xin = self.IOB[:, slot * 1024: slot * 1024 + TB]
            b = self.b_iob[slot]
            self.p.acquire("sync", b)
            self.dma("sync", xin, x_hbm[k * 128:(k + 1) * 128, tok0:tok0 + TB], b)
            self.p.wait_buf("vector", b)
            if k == 0:
                self.wait("vector", self.s_t, t_bc)
            xnk = XN[:, k * TB:(k + 1) * TB]
            self.op("vector", "tensor_tensor", xnk, xin, self.PSC[:, 0:TB], ALU.subtract)
            self.op("vector", "tensor_tensor", xnk, xnk, self.PSD[:, 0:TB], ALU.mult)
            self.inc("vector", self.s_v, 1)
            self._v_xn_chunk[k] = self.s_v.n
            self.p.free_buf(None, b, self.s_v)
        self.b_ps["C"].frees.append((self.s_v, self.s_v.n))
        self.b_ps["D"].frees.append((self.s_v, self.s_v.n))
        return self.s_v.n

    def _ffn(self, x_hbm, dst_hbm, collect_halo):
        T = self.T
        if not collect_halo:
            for e in ENGINES:
                self.wait(e, self.s_t, self.s_t.n)
                self.wait(e, self.s_v, self.s_v.n)
                self.wait(e, self.s_s, self.s_s.n)
        tok0 = 0
        for bi, TB in enumerate(self.FFN_BLKS):
            self._ffn_block(x_hbm, dst_hbm, tok0, TB,
                            halo=(collect_halo and tok0 + TB == T))
            tok0 += TB
        if collect_halo:
            self._halo_exchange()

    def _ffn_block(self, x_hbm, dst_hbm, tok0, TB, halo):
        nck = TB // 512
        FUSE = self.ARN[:, 0:48 * 1024]
        XN = self.ARN[:, 49152:49152 + 16 * TB]
        v_xn = self._layernorm_to_xn(x_hbm, tok0, TB, XN)

        def gu_cast(jj):
            self.wcast(self.d_w8gu[jj][:, 0:2048],
                       self.din["w_gate"][jj].rearrange("f k d -> f (k d)"), jj % 2)
            self._castv[("gu", jj)] = self.wcast(
                self.d_w8gu[jj][:, 2048:4096],
                self.din["w_up"][jj].rearrange("f k d -> f (k d)"), jj % 2)
        for b in self.b_wr:
            self.p.acquire("gpsimd", b)
        for j in range(48):
            wslot = j % 3
            slot = j % 2
            wgu = self.WR[:, wslot * 4096: wslot * 4096 + 4096]
            bw = self.b_wr[wslot]
            self.p.acquire("gpsimd", bw)
            if not self._c_ffn:
                if j == 0:
                    gu_cast(0)
                if j + 1 < 48:
                    gu_cast(j + 1)
            self.wait("gpsimd", *self._castv[("gu", j)])
            self.dma("gpsimd", wgu, self.d_w8gu[j], bw)
            wgt = wgu[:, 0:2048]
            wut = wgu[:, 2048:4096]
            psg, psu = ("A", "B") if slot == 0 else ("C", "D")
            pg = {"A": self.PSA, "B": self.PSB, "C": self.PSC, "D": self.PSD}
            self.p.acquire("tensor", self.b_ps[psg])
            self.p.acquire("tensor", self.b_ps[psu])
            self.p.wait_buf("tensor", bw)
            for k in range(16):
                if j == 0:
                    self.wait("tensor", self.s_v, self._v_xn_chunk[k])
                lg = wgt[:, k * 128:(k + 1) * 128]
                lu = wut[:, k * 128:(k + 1) * 128]
                for h in range(nck):
                    rhs = XN[:, k * TB + h * 512: k * TB + h * 512 + 512]
                    self.op("tensor", "matmul", pg[psg][:, h * 512:(h + 1) * 512],
                            lg, rhs, start=(k == 0), stop=(k == 15))
                    self.op("tensor", "matmul", pg[psu][:, h * 512:(h + 1) * 512],
                            lu, rhs, start=(k == 0), stop=(k == 15))
            self.inc("tensor", self.s_t, 1)
            bw.frees.append((self.s_t, self.s_t.n))
            bs = self.b_stg[slot]
            self.p.acquire2(("scalar", "vector"), bs)
            self.wait("scalar", self.s_t, self.s_t.n)
            gt = self.STG[:, slot * 2048: slot * 2048 + TB]
            self.op("scalar", "activation", gt, pg[psg][:, 0:TB], AF.Silu,
                    bias=0.0, scale=self.GSC[:, j:j + 1])
            self.inc("scalar", self.s_s, 1)
            ut = self.STG[:, slot * 2048 + 1024: slot * 2048 + 1024 + TB]
            self.wait("vector", self.s_t, self.s_t.n)
            self.op("vector", "tensor_scalar", ut, pg[psu][:, 0:TB],
                    self.USC[:, j:j + 1], None, ALU.mult)
            self.wait("vector", self.s_s, self.s_s.n)
            self.op("vector", "tensor_tensor", FUSE[:, j * 1024: j * 1024 + TB], gt, ut, ALU.mult)
            self.inc("vector", self.s_v, 1)
            bs.frees.append((self.s_v, self.s_v.n))
            self.b_ps[psg].frees.append((self.s_s, self.s_s.n))
            self.b_ps[psu].frees.append((self.s_v, self.s_v.n))
        v_fuse = self.s_v.n

        def dn_cast(mm):
            self._castv[("dn", mm)] = self.wcast(
                self.d_w8d[mm],
                self.din["w_down"][mm].rearrange("f c d -> f (c d)"), mm % 2)
        for b in self.b_wr:
            self.p.acquire("gpsimd", b)
        for m in range(16):
            slot = m % 2
            wslot = self.WR[:, slot * 6144:(slot + 1) * 6144]
            bw = self.b_wr[slot]
            self.p.acquire("gpsimd", bw)
            if not self._c_ffn:
                if m == 0:
                    dn_cast(0)
                if m + 1 < 16:
                    dn_cast(m + 1)
            self.wait("gpsimd", *self._castv[("dn", m)])
            self.dma("gpsimd", wslot[:, 0:6144], self.d_w8d[m], bw)
            psd = "A" if slot == 0 else "C"
            pg = {"A": self.PSA, "C": self.PSC}
            self.p.acquire("tensor", self.b_ps[psd])
            xres = self.IOB[:, slot * 1024: slot * 1024 + TB]
            b2 = self.b_iob[slot]
            self.p.acquire("sync", b2)
            self.dma("sync", xres, x_hbm[m * 128:(m + 1) * 128, tok0:tok0 + TB], b2)
            self.p.wait_buf("tensor", bw)
            if m == 0:
                self.wait("tensor", self.s_v, v_fuse)
            for c in range(48):
                lw = wslot[:, c * 128:(c + 1) * 128]
                for h in range(nck):
                    self.op("tensor", "matmul", pg[psd][:, h * 512:(h + 1) * 512],
                            lw, FUSE[:, c * 1024 + h * 512: c * 1024 + h * 512 + 512],
                            start=(c == 0), stop=(c == 47))
            self.inc("tensor", self.s_t, 1)
            bw.frees.append((self.s_t, self.s_t.n))
            self.wait("vector", self.s_t, self.s_t.n)
            self.p.wait_buf("vector", b2)
            rout = xres
            self.op("vector", "scalar_tensor_tensor", rout, pg[psd][:, 0:TB],
                    self.DSH[:, m:m + 1], xres, ALU.mult, ALU.add)
            self.inc("vector", self.s_v, 1)
            b2.ready = (self.s_v, self.s_v.n)
            self.b_ps[psd].frees.append((self.s_v, self.s_v.n))
            self.p.wait_buf("sync", b2)
            self.dma("sync", dst_hbm[m * 128:(m + 1) * 128, tok0:tok0 + TB], rout, b2)
            b2.frees.append((b2.sem, b2.sem.n))
            if halo:
                self.dma("sync", self.d_halo_in[m * 128:(m + 1) * 128, 0:4],
                         rout[:, TB - 4:TB], b2)
                b2.frees.append((b2.sem, b2.sem.n))
        self._c_ffn = True

    def _halo_exchange(self):
        self.wait("gpsimd", self.b_iob[0].sem, self.b_iob[0].sem.n)
        self.wait("gpsimd", self.b_iob[1].sem, self.b_iob[1].sem.n)
        self.op("gpsimd", "collective_compute", "AllGather", ALU.bypass,
                replica_groups=self.pair_groups,
                ins=[self.d_halo_in.opt()], outs=[self.d_halo_out.opt()])
        self.inc("gpsimd", self.s_cc, 1)
        self.wait("sync", self.s_cc, self.s_cc.n)
        self.dma("sync", self.HALO[:, 0:16 * 3].rearrange("p (k c) -> p k c", c=3),
                 self.d_halo_out[0:D, 1:4].rearrange("(k p) c -> p k c", p=128), self.b_halo)
        self.p.wait_buf("vector", self.b_halo)
        self.op("vector", "tensor_scalar", self.HALO[:, 0:48], self.HALO[:, 0:48],
                self.GAIN[:, 0:1], None, ALU.mult)
        self.inc("vector", self.s_v, 1)
        self.v_halo = self.s_v.n



    @staticmethod
    def apx(base2d, dims):
        ap = [list(base2d.ap[0])] + [[s, c] for (s, c) in dims]
        return bass.AP(base2d.tensor, base2d.offset, ap)

    def _halo_ln(self):
        bs = self.b_stg[0]
        self.p.acquire2(("vector", "scalar"), bs)
        self.wait("vector", self.s_v, self.v_halo)
        hb = self.STG[:, 0:48]
        self.op("vector", "tensor_copy", hb, self.HALO[:, 0:48])
        self.inc("vector", self.s_v, 1)
        sqh = self.STG[:, 64:112]
        self.wait("scalar", self.s_v, self.v_halo)
        self.op("scalar", "activation", sqh, self.HALO[:, 0:48], AF.Square, bias=0.0, scale=1.0)
        self.inc("scalar", self.s_s, 1)
        self.p.acquire("tensor", self.b_ps["A"])
        self.p.acquire("tensor", self.b_ps["B"])
        self.wait("tensor", self.s_v, self.s_v.n)
        self.wait("tensor", self.s_s, self.s_s.n)
        for k in range(16):
            self.op("tensor", "matmul", self.PSA[0:1, 0:3], self.ONCB[:, :],
                    hb[:, 3 * k:3 * k + 3], start=(k == 0), stop=(k == 15))
            self.op("tensor", "matmul", self.PSB[0:1, 0:3], self.ONCB[:, :],
                    sqh[:, 3 * k:3 * k + 3], start=(k == 0), stop=(k == 15))
        self.inc("tensor", self.s_t, 1)
        bs.frees.append((self.s_t, self.s_t.n))
        rows = self.ROWS
        mu = rows[0:1, 0:3]
        msq = rows[0:1, 8:11]
        rstd = rows[0:1, 16:19]
        self.wait("vector", self.s_t, self.s_t.n)
        self.op("vector", "tensor_scalar", mu, self.PSA[0:1, 0:3], 1.0 / D, None, ALU.mult)
        self.inc("vector", self.s_v, 1)
        self.wait("scalar", self.s_v, self.s_v.n)
        self.op("scalar", "activation", msq, mu, AF.Square, bias=0.0, scale=1.0)
        self.inc("scalar", self.s_s, 1)
        self.wait("vector", self.s_s, self.s_s.n)
        self.op("vector", "scalar_tensor_tensor", msq, self.PSB[0:1, 0:3], 1.0 / D, msq,
                ALU.mult, ALU.subtract)
        self.inc("vector", self.s_v, 1)
        self.b_ps["A"].frees.append((self.s_v, self.s_v.n))
        self.b_ps["B"].frees.append((self.s_v, self.s_v.n))
        self.wait("scalar", self.s_v, self.s_v.n)
        self.op("scalar", "activation", msq, msq, AF.Ln, bias=self.ACONST[0:1, 0:1], scale=1.0)
        self.schain()
        self.op("scalar", "activation", rstd, msq, AF.Exp, bias=0.0, scale=self.ACONST[0:1, 1:2])
        self.inc("scalar", self.s_s, 1)
        self.p.acquire("tensor", self.b_ps["C"])
        self.p.acquire("tensor", self.b_ps["D"])
        self.wait("tensor", self.s_s, self.s_s.n)
        self.op("tensor", "matmul", self.PSC[:, 0:3], self.ONR[:, :], mu, start=True, stop=True)
        self.op("tensor", "matmul", self.PSD[:, 0:3], self.ONR[:, :], rstd, start=True, stop=True)
        self.inc("tensor", self.s_t, 1)
        self.wait("vector", self.s_t, self.s_t.n)
        xnh3 = self.XNH[:, 0:48].rearrange("p (k c) -> p k c", c=3)
        halo3 = self.HALO[:, 0:48].rearrange("p (k c) -> p k c", c=3)
        muB = self.PSC[:, 0:3].unsqueeze(1).broadcast_to([128, 16, 3])
        rsB = self.PSD[:, 0:3].unsqueeze(1).broadcast_to([128, 16, 3])
        self.op("vector", "tensor_tensor", xnh3, halo3, muB, ALU.subtract)
        self.op("vector", "tensor_tensor", xnh3, xnh3, rsB, ALU.mult)
        self.inc("vector", self.s_v, 1)
        self.b_ps["C"].frees.append((self.s_v, self.s_v.n))
        self.b_ps["D"].frees.append((self.s_v, self.s_v.n))
        self.v_xnh = self.s_v.n

    def _mamba_front(self):
        A = self.ARN
        XBC = A[:, 0:48 * 512]
        self.XBC = XBC
        XN = A[:, 24720:24720 + 8192]
        self.PREV = A[:, 32912:32912 + 8192].bitcast(F32)
        fr = dict(
            XBC=XBC, XN=XN,
            PREVB=A[:, 41104:43152],
            XT=A[:, 43152:44176],
            BT=A[:, 44176:45200],
            XDT=A[:, 45200:49296],
            DECT=A[:, 49296:49424].bitcast(F32),
            CDB=A[:, 49424:49552].bitcast(F32),
            EXPC0B=A[:, 49552:49680].bitcast(F32),
            E8=[A[:, 24720:26768].bitcast(F32),
                A[:, 26768:28816].bitcast(F32)],
            GL8=[A[:, 51728:52752], A[:, 50704:51728]],
            CHX8=[A[:, 52752:53776], A[:, 28816:29840]],
            XDTD8=[A[:, 53776:54288], A[:, 64528:65040]],
            ACC=A[:, 54288:55312].bitcast(F32),
            TSCR=A[:, 55312:56336].bitcast(F32),
            YST=A[:, 56336:56336 + 8192].bitcast(F32),
        )
        fr["DTRM"] = fr["ACC"]
        self.XNH = A[:, 50704:50704 + 64]
        fr["DAFM"] = A[:, 49680:49680 + 256].bitcast(F32)[0:64, 0:128]
        fr["ACCP"] = [fr["ACC"], fr["TSCR"]]
        self._fr = fr
        self.b_fr = {k: Buf(self.p) for k in ("TSCR", "DTRM", "YST")}
        self.b_e8 = [Buf(self.p), Buf(self.p)]
        self.b_gl8 = [Buf(self.p), Buf(self.p)]
        self.b_chx8 = [Buf(self.p), Buf(self.p)]
        self.b_xdtd = [Buf(self.p), Buf(self.p)]
        self.b_prevb = [Buf(self.p), Buf(self.p)]
        self.b_psby = Buf(self.p)
        self.b_psbx = Buf(self.p)
        self.b_pscg = Buf(self.p)
        self.b_pscs = Buf(self.p)
        for q in range(4):
            self.dma("sync", self.PREV[:, q * 1024:(q + 1) * 1024],
                     self.din["c_zeros"][:, :], self.s_x)
        self.v_prev0 = (self.s_x, self.s_x.n)

        self._halo_ln()
        for b in range(self.NMB):
            self._mamba_block(b)
        self.b_carry_out = Buf(self.p)
        self.wait("sync", self.s_v, self.s_v.n)
        self.dma("sync", self.d_carry_snd[:, :], self.PREV[:, :], self.b_carry_out)
        self.wait("gpsimd", self.b_carry_out.sem, self.b_carry_out.sem.n)
        self.op("gpsimd", "collective_compute", "AllGather", ALU.bypass,
                replica_groups=self.pair_groups,
                ins=[self.d_carry_snd.opt()], outs=[self.d_carry_rcv.opt()])
        self.inc("gpsimd", self.s_cc, 1)

    def _mamba_block(self, b):
        fr = self._fr
        tok0 = b * 512
        XN = fr["XN"]
        if b > 0:
            for e in ("scalar", "vector"):
                self.wait(e, self.s_t, self.s_t.n)
                self.wait(e, self.s_v, self.s_v.n)
                self.wait(e, self.s_spc, self.s_spc.n)
        v_xn = self._layernorm_to_xn(self.d_r1, tok0, 512, XN)

        psl = [self.PSA, self.PSB, self.PSC, self.PSD]
        psn = ["A", "B", "C", "D"]
        def in_cast(jj):
            wl = 1024 if jj == 80 else 2048
            s_ = (self.din["w_indt"].rearrange("f k d -> f (k d)") if jj == 80
                  else self.din["w_inT"][jj].rearrange("f k d -> f (k d)"))
            self._castv[("in", jj)] = self.wcast(self.d_wbin[jj][:, 0:wl], s_, jj % 2)
        for bwb in self.b_wr:
            self.p.acquire("gpsimd", bwb)
        for j in range(81):
            slot = j % 2
            wslot = j % 6
            is_dt = (j == 80)
            wlen = 1024 if is_dt else 2048
            wt = self.WR[:, wslot * 2048: wslot * 2048 + wlen]
            bw = self.b_wr[wslot]
            self.p.acquire("gpsimd", bw)
            if not self._c_inp:
                if j == 0:
                    in_cast(0)
                if j + 1 < 81:
                    in_cast(j + 1)
            self.wait("gpsimd", *self._castv[("in", j)])
            self.dma("gpsimd", wt, self.d_wbin[j][:, 0:wlen], bw)
            pb = psn[j % 4]
            ps = psl[j % 4]
            self.p.acquire("tensor", self.b_ps[pb])
            halo = (b == 0 and (32 <= j < 80))
            self.p.wait_buf("tensor", bw)
            if j == 0 and b == 0:
                self.wait("tensor", self.s_v, self.v_xnh)
            wd = 64 if is_dt else 128
            rows = 64 if is_dt else 128
            for k in range(16):
                if j == 0:
                    self.wait("tensor", self.s_v, self._v_xn_chunk[k])
                lw = wt[:, k * wd:(k + 1) * wd]
                self.op("tensor", "matmul", ps[0:rows, 0:512],
                        lw, XN[:, k * 512:(k + 1) * 512], start=(k == 0), stop=(k == 15))
                if halo:
                    self.op("tensor", "matmul", ps[0:rows, 512:515],
                            lw, self.XNH[:, k * 3:k * 3 + 3], start=(k == 0), stop=(k == 15))
            self.inc("tensor", self.s_t, 1)
            bw.frees.append((self.s_t, self.s_t.n))
            if j < 32:
                bs = self.b_stg[slot]
                self.p.acquire("scalar", bs)
                zt = self.STG[:, slot * 2048: slot * 2048 + 512]
                self.wait("scalar", self.s_t, self.s_t.n)
                self.op("scalar", "activation", zt, ps[:, 0:512], AF.Silu, bias=0.0, scale=1.0)
                self.inc("scalar", self.s_s, 1)
                self.b_ps[pb].frees.append((self.s_s, self.s_s.n))
                self.wait("sync", self.s_s, self.s_s.n)
                for c in range(4):
                    base = (b * 4 + c) * 4096 + j * 128
                    self.dma("sync", self.d_sz[:, base:base + 128],
                             zt[:, c * 128:(c + 1) * 128], bs)
                bs.frees.append((bs.sem, bs.sem.n))
            elif j < 80:
                t = j - 32
                if halo:
                    self.wait("vector", self.s_t, self.s_t.n)
                    self.op("vector", "tensor_copy", self.XBCH[:, t * 3:t * 3 + 3],
                            ps[:, 512:515])
                    self.inc("vector", self.s_v, 1)
                    self.b_ps[pb].frees.append((self.s_v, self.s_v.n))
                self._conv_tile(b, t, ps, pb)
            else:
                self.wait("vector", self.s_t, self.s_t.n)
                self.wait("vector", self.s_s, self.s_s.n)
                t0 = fr["ACC"][0:64, 0:512]
                t1 = fr["TSCR"][0:64, 0:512]
                self.p.acquire("vector", self.b_fr["TSCR"])
                self.op("vector", "tensor_scalar", t0, ps[0:64, 0:512],
                        self.DTBias[:, 0:1], None, ALU.add)
                self.inc("vector", self.s_v, 1)
                self.b_ps[pb].frees.append((self.s_v, self.s_v.n))
                self.wait("scalar", self.s_v, self.s_v.n)
                self.op("scalar", "activation", t1, t0, AF.Abs, bias=0.0, scale=1.0)
                self.schain()
                self.op("scalar", "activation", t1, t1, AF.Exp, bias=0.0, scale=-1.0)
                self.schain()
                self.op("scalar", "activation", t1, t1, AF.Ln, bias=1.0, scale=1.0)
                self.inc("scalar", self.s_s, 1)
                self.wait("vector", self.s_s, self.s_s.n)
                dtf = self.DTF[:, tok0:tok0 + 512]
                self.op("vector", "tensor_scalar", t0, t0, 0.0, None, ALU.max)
                self.op("vector", "tensor_tensor", dtf, t0, t1, ALU.add)
                self.inc("vector", self.s_v, 1)
                self.b_fr["TSCR"].frees.append((self.s_v, self.s_v.n))

        self._c_inp = True
        s_conv = self.s_s.n

        for c_loc in range(4):
            self._ssd_chunk(b, c_loc, s_conv)
        self._fr_psum_handoff()

    def _conv_tile(self, b, t, ps, pb):
        fr = self._fr
        tile = fr["XBC"][:, t * 512: t * 512 + 512]
        lead = self.XBCH[:, t * 3:t * 3 + 3] if b == 0 else self.TAIL[:, t * 3:t * 3 + 3]
        acc = fr["ACCP"][t % 2][:, 0:512]
        if t % 2 == 1:
            self.p.acquire("vector", self.b_fr["TSCR"])
        if t >= 2:
            self.wait("vector", self.s_s, self._conv_s[t % 2])
        self.wait("vector", self.s_t, self.s_t.n)
        self.op("vector", "tensor_scalar", acc, ps[:, 0:512],
                self.CONVW[:, 4 * t + 3:4 * t + 4], self.CONVB[:, t:t + 1],
                ALU.mult, ALU.add)
        for kk in range(3):
            off = 3 - kk
            self.op("vector", "scalar_tensor_tensor", acc[:, off:512],
                    ps[:, 0:512 - off],
                    self.CONVW[:, 4 * t + kk:4 * t + kk + 1], acc[:, off:512],
                    ALU.mult, ALU.add)
            self.op("vector", "scalar_tensor_tensor", acc[:, 0:off],
                    lead[:, kk:3],
                    self.CONVW[:, 4 * t + kk:4 * t + kk + 1], acc[:, 0:off],
                    ALU.mult, ALU.add)
        self.op("vector", "tensor_copy", self.TAIL[:, t * 3:t * 3 + 3], ps[:, 509:512])
        self.inc("vector", self.s_v, 1)
        self.b_ps[pb].frees.append((self.s_v, self.s_v.n))
        self.wait("scalar", self.s_v, self.s_v.n)
        self.op("scalar", "activation", tile[:, 0:512], acc, AF.Silu, bias=0.0, scale=1.0)
        self.inc("scalar", self.s_s, 1)
        if not hasattr(self, "_conv_s"):
            self._conv_s = {}
        self._conv_s[t % 2] = self.s_s.n
        if t % 2 == 1:
            self.b_fr["TSCR"].frees.append((self.s_s, self.s_s.n))
        if t >= 16:
            self.wait("sync", self.s_s, self.s_s.n)
            for c in range(4):
                base = (b * 4 + c) * 4096 + (t - 16) * 128
                self.dma("sync", self.d_c[:, base:base + 128],
                         tile[:, c * 128:(c + 1) * 128], self.s_spc)

    def _ssd_chunk(self, b, c_loc, s_conv):
        fr = self._fr
        ch = b * 4 + c_loc
        col0 = c_loc * 128
        XBC = fr["XBC"]

        def xbc_tile(t):
            return XBC[:, t * 512 + col0: t * 512 + col0 + 128]

        dafm = fr["DAFM"]
        self.op("vector", "tensor_scalar", dafm, self.DTF[:, ch * 128:(ch + 1) * 128],
                self.ACOL[:, 0:1], None, ALU.mult)
        self.inc("vector", self.s_v, 1)
        v_dafm = self.s_v.n
        self.p.acquire("tensor", self.b_ps["D"])
        self.wait("tensor", self.s_v, v_dafm)
        ps_daT = self.PSD[:, 512:576]
        ps_dtT = self.PSD[:, 576:640]
        self.op("tensor", "transpose", ps_daT, dafm, self.IDF[0:64, 0:64])
        self.op("tensor", "transpose", ps_dtT, self.DTF[:, ch * 128:(ch + 1) * 128],
                self.IDF[0:64, 0:64])
        self.inc("tensor", self.s_t, 1)
        t_tr = self.s_t.n
        dah = self.DAHL[:, ch * 64:(ch + 1) * 64]
        dal = self.DAHL[:, (self.NCH + ch) * 64:(self.NCH + ch + 1) * 64]
        self.wait("vector", self.s_t, t_tr)
        self.op("vector", "tensor_copy", dah, ps_daT)
        self.op("vector", "tensor_tensor", dal, ps_daT, dah, ALU.subtract)
        self.inc("vector", self.s_v, 1)
        v_dahl = self.s_v.n
        ps_cst = self.PSD[:, 640:704]
        ps_dec = self.PSD[:, 704:768]
        ps_cdf = self.PSD[:, 768:832]
        self.wait("tensor", self.s_v, v_dahl)
        for (lhs, pso) in ((self.ULEB, ps_cst), (self.SLTB, ps_dec), (self.ONES128B, ps_cdf)):
            self.op("tensor", "matmul", pso, lhs[:, :], dah, start=True, stop=False)
            self.op("tensor", "matmul", pso, lhs[:, :], dal, start=False, stop=True)
        self.inc("tensor", self.s_t, 1)
        t_mm = self.s_t.n
        cstn = self.CSTN[:, ch * 64:(ch + 1) * 64]
        self.wait("vector", self.s_t, t_mm)
        self.op("vector", "tensor_scalar", cstn, ps_cst, -1.0, None, ALU.mult)
        self.inc("vector", self.s_v, 1)
        v_cstn = self.s_v.n
        ps_cs0 = self.PSD[:, 832:896]
        self.op("tensor", "matmul", ps_cs0, self.E0B[:, :], dah, start=True, stop=False)
        self.op("tensor", "matmul", ps_cs0, self.E0B[:, :], dal, start=False, stop=True)
        self.inc("tensor", self.s_t, 1)
        t_mm = self.s_t.n
        if self.dbg_ssd and ch == 1:
            scr = fr["TSCR"][:, 0:64]
            self.wait("vector", self.s_t, t_mm)
            self.wait("vector", self.s_v, self.s_v.n)
            self.op("vector", "tensor_copy", scr, ps_cs0)
            self.inc("vector", self.s_v, 1)
            self._dbg_snap("dbg_cs0ps", scr)
        self.wait("scalar", self.s_t, t_mm)
        self.op("scalar", "activation", fr["DECT"][:, 0:64], ps_dec, AF.Exp, bias=0.0, scale=1.0)
        self.op("scalar", "activation", fr["CDB"][:, 0:64], ps_cdf, AF.Exp, bias=0.0, scale=1.0)
        self.op("scalar", "activation", fr["EXPC0B"][:, 0:64], ps_cs0, AF.Exp, bias=0.0, scale=1.0)
        self.inc("scalar", self.s_s, 1)
        s_decb = self.s_s.n
        self.wait("sync", self.s_v, v_cstn)
        self.dma("sync", self.CLOG[0:1, ch * 64:(ch + 1) * 64], cstn[127:128, :], self.b_clog)
        psd6 = self.PSD[:, 0:512].bitcast(BF16)
        self.wait("tensor", self.s_s, s_conv)
        for g in range(8):
            self.op("tensor", "transpose", psd6[:, g * 128:(g + 1) * 128],
                    xbc_tile(g), self.IDB[:, :])
        self.inc("tensor", self.s_t, 1)
        self.wait("vector", self.s_t, self.s_t.n)
        self.op("vector", "tensor_copy", fr["XT"][:, :], psd6[:, :])
        self.inc("vector", self.s_v, 1)
        self.wait("tensor", self.s_v, self.s_v.n)
        for g in range(8):
            self.op("tensor", "transpose", psd6[:, g * 128:(g + 1) * 128],
                    xbc_tile(8 + g), self.IDB[:, :])
        self.inc("tensor", self.s_t, 1)
        self.wait("vector", self.s_t, self.s_t.n)
        self.op("vector", "tensor_copy", fr["BT"][:, :], psd6[:, :])
        self.inc("vector", self.s_v, 1)
        v_xbt = self.s_v.n
        xt_rep = self.apx(fr["XT"][:, 0:1024], [(128, 8), (0, 4), (64, 2), (1, 64)])
        dtt_rep = self.apx(ps_dtT, [(8, 8), (2, 4), (1, 2), (0, 64)])
        xdt5 = fr["XDT"][:, :].rearrange("p (a r1 r0 k) -> p a r1 r0 k", a=8, r1=4, r0=2)
        self.op("vector", "tensor_tensor", xdt5, xt_rep, dtt_rep, ALU.mult)
        self.inc("vector", self.s_v, 1)
        v_xdt = self.s_v.n
        self.b_ps["D"].frees.append((self.s_v, self.s_v.n))
        if ch == 0:
            self.wait("vector", *self.v_prev0)
        if self.dbg_ssd and ch == 1:
            self._dbg_snap("dbg_prev", self.PREV[:, :])
            self._dbg_snap("dbg_xdt", fr["XDT"][:, :])
            self._dbg_snap("dbg_cstn", self.CSTN[:, 64:128])
            self._dbg_snap("dbg_dah", self.DAHL[:, 64:128])
            self._dbg_snap("dbg_dal",
                           self.DAHL[:, (self.NCH + 1) * 64:(self.NCH + 2) * 64])
            self._dbg_snap("dbg_dect", fr["DECT"][:, 0:64])
            self._dbg_snap("dbg_expc0", fr["EXPC0B"][:, 0:64])

        self.p.acquire("vector", self.b_fr["YST"])
        self._frst = {}
        self._fr_A(ch, c_loc, 0, v_dahl, v_cstn)
        self._fr_B(ch, c_loc, 0, v_xdt, s_decb)
        for g in range(1, 8):
            self._fr_A(ch, c_loc, g, v_dahl, v_cstn)
            self._fr_C(ch, c_loc, g - 1)
            self._fr_B(ch, c_loc, g, v_xdt, s_decb)
        self._fr_C(ch, c_loc, 7)

        self.wait("sync", self.s_v, self.s_v.n)
        yb = self.b_fr["YST"]
        self.dma("sync", self.d_yp[:, ch * 4096:(ch + 1) * 4096], fr["YST"][:, :], yb)
        yb.frees.append((yb.sem, yb.sem.n))

    def _fr_A(self, ch, c_loc, g, v_dahl, v_cstn):
        fr = self._fr
        h0 = 8 * g
        j0 = 4 * g
        col0 = c_loc * 128
        XBC = fr["XBC"]
        sl = g % 2
        self.p.acquire("tensor", self.b_ps["A"])
        self.wait("tensor", self.s_v, v_dahl)
        for i in range(8):
            h = h0 + i
            dah = self.DAHL[:, ch * 64 + h:ch * 64 + h + 1]
            dal = self.DAHL[:, (self.NCH + ch) * 64 + h:(self.NCH + ch) * 64 + h + 1]
            self.op("tensor", "matmul", self.PSA[:, i * 128:(i + 1) * 128],
                    dah.broadcast_to([128, 128]), self.ULEB[:, :],
                    start=(i == 0 or i == 4), stop=False)
            self.op("tensor", "matmul", self.PSA[:, i * 128:(i + 1) * 128],
                    dal.broadcast_to([128, 128]), self.ULEB[:, :],
                    start=False, stop=False)
            self.op("tensor", "matmul", self.PSA[:, i * 128:(i + 1) * 128],
                    self.IDB[:, :], self.MASKN[:, :], start=False, stop=(i == 3 or i == 7))
        self.inc("tensor", self.s_t, 1)
        t_mask = self.s_t.n
        e8 = fr["E8"][sl]
        self.p.acquire("scalar", self.b_e8[sl])
        self.wait("scalar", self.s_t, t_mask)
        self.wait("scalar", self.s_v, v_cstn)
        for i in range(8):
            h = h0 + i
            self.op("scalar", "activation", e8[:, i * 128:(i + 1) * 128],
                    self.PSA[:, i * 128:(i + 1) * 128], AF.Exp,
                    bias=self.CSTN[:, ch * 64 + h:ch * 64 + h + 1], scale=1.0)
        self.inc("scalar", self.s_s, 1)
        s_e8 = self.s_s.n
        self.b_ps["A"].frees.append((self.s_s, s_e8))
        if self.dbg_ssd and ch == 1 and g == 7:
            self._dbg_snap("dbg_e8", e8[:, :])
        self.p.acquire("tensor", self.b_pscg)
        for q in range(4):
            j = j0 + q
            self.op("tensor", "matmul", self.PSC[:, q * 128:(q + 1) * 128],
                    XBC[:, (8 + j // 4) * 512 + col0:(8 + j // 4) * 512 + col0 + 128],
                    XBC[:, (16 + j) * 512 + col0:(16 + j) * 512 + col0 + 128],
                    start=(q == 0), stop=(q == 3))
        self.inc("tensor", self.s_t, 1)
        self._frst[g] = dict(s_e8=s_e8, t_g=self.s_t.n)

    def _fr_B(self, ch, c_loc, g, v_xdt, s_decb):
        fr = self._fr
        st = self._frst[g]
        h0 = 8 * g
        j0 = 4 * g
        col0 = c_loc * 128
        XBC = fr["XBC"]
        sl = g % 2
        e8 = fr["E8"][sl]
        gl8 = fr["GL8"][sl]
        self.p.acquire("vector", self.b_gl8[sl])
        self.wait("vector", self.s_t, st["t_g"])
        self.wait("vector", self.s_s, st["s_e8"])
        psc_rep = self.apx(self.PSC[:, 0:512], [(128, 4), (0, 2), (1, 128)])
        self.op("vector", "tensor_tensor",
                gl8[:, :].rearrange("p (q h l) -> p q h l", q=4, h=2),
                e8[:, :].rearrange("p (q h l) -> p q h l", q=4, h=2),
                psc_rep, ALU.mult)
        self.inc("vector", self.s_v, 1)
        v_gl = self.s_v.n
        self.b_e8[sl].frees.append((self.s_v, v_gl))
        self.b_pscg.frees.append((self.s_v, v_gl))
        if self.dbg_ssd and ch == 1 and g == 7:
            self._dbg_snap("dbg_gl8", gl8[:, :])
        self.p.acquire("tensor", self.b_psbx)
        self.p.acquire("tensor", self.b_ps["D"])
        self.wait("tensor", self.s_s, st["s_e8"])
        ps_r = [self.PSB[:, 512:1024], self.PSD[:, 0:512]]
        for r in range(2):
            for q in range(4):
                i = 4 * r + q
                self.op("tensor", "matmul", ps_r[r][:, q * 128:(q + 1) * 128],
                        self.ONR[:, :], e8[0:1, i * 128:(i + 1) * 128],
                        start=True, stop=True)
        self.inc("tensor", self.s_t, 1)
        t_chx = self.s_t.n
        self.b_e8[sl].frees.append((self.s_t, t_chx))
        chx8 = fr["CHX8"][sl]
        self.p.acquire("vector", self.b_chx8[sl])
        self.wait("vector", self.s_t, t_chx)
        for r in range(2):
            c_base = XBC[:, (16 + j0 + 2 * r) * 512 + col0:(16 + j0 + 2 * r) * 512 + col0 + 128]
            c_rep = self.apx(c_base, [(512, 2), (0, 2), (1, 128)])
            self.op("vector", "tensor_tensor",
                    chx8[:, r * 512:(r + 1) * 512].rearrange(
                        "p (q h l) -> p q h l", q=2, h=2),
                    ps_r[r].rearrange("p (q h l) -> p q h l", q=2, h=2),
                    c_rep, ALU.mult)
        self.inc("vector", self.s_v, 1)
        v_chx = self.s_v.n
        self.b_psbx.frees.append((self.s_v, v_chx))
        self.b_ps["D"].frees.append((self.s_v, v_chx))
        if self.dbg_ssd and ch == 1 and g == 7:
            self._dbg_snap("dbg_chx", chx8[:, :])
        xdtd = fr["XDTD8"][sl]
        self.p.acquire("vector", self.b_xdtd[sl])
        self.wait("vector", self.s_v, v_xdt)
        self.wait("vector", self.s_s, s_decb)
        dect_rep = self.apx(fr["DECT"][:, h0:h0 + 8], [(1, 8), (0, 64)])
        self.op("vector", "tensor_tensor",
                xdtd[:, :].rearrange("p (h k) -> p h k", h=8),
                fr["XDT"][:, g * 512:(g + 1) * 512].rearrange("p (h k) -> p h k", h=8),
                dect_rep, ALU.mult)
        self.inc("vector", self.s_v, 1)
        v_xdtd = self.s_v.n
        prevbg = fr["PREVB"][:, sl * 512:sl * 512 + 512]
        self.p.acquire("vector", self.b_prevb[sl])
        exc0_rep = self.apx(fr["EXPC0B"][:, h0:h0 + 8], [(1, 8), (0, 64)])
        self.op("vector", "tensor_tensor",
                prevbg.rearrange("p (h k) -> p h k", h=8),
                self.PREV[:, g * 512:(g + 1) * 512].rearrange("p (h k) -> p h k", h=8),
                exc0_rep, ALU.mult)
        self.inc("vector", self.s_v, 1)
        st.update(v_gl=v_gl, v_chx=v_chx, v_xdtd=v_xdtd, v_prevbg=self.s_v.n)
        if self.dbg_ssd and ch == 1 and g == 7:
            self._dbg_snap("dbg_prevb", prevbg)

    def _fr_C(self, ch, c_loc, g):
        fr = self._fr
        st = self._frst[g]
        h0 = 8 * g
        j0 = 4 * g
        col0 = c_loc * 128
        XBC = fr["XBC"]
        sl = g % 2
        xdtd = fr["XDTD8"][sl]
        gl8 = fr["GL8"][sl]
        chx8 = fr["CHX8"][sl]
        prevbg = fr["PREVB"][:, sl * 512:sl * 512 + 512]
        self.p.acquire("tensor", self.b_pscs)
        self.wait("tensor", self.s_v, st["v_xdtd"])
        self.op("tensor", "matmul", self.PSC[:, 512:1024],
                fr["BT"][:, g * 128:(g + 1) * 128], xdtd[:, :],
                start=True, stop=True)
        self.inc("tensor", self.s_t, 1)
        t_s8 = self.s_t.n
        self.b_xdtd[sl].frees.append((self.s_t, t_s8))
        self.p.acquire("tensor", self.b_psby)
        self.wait("tensor", self.s_v, st["v_gl"])
        self.wait("tensor", self.s_v, st["v_chx"])
        self.wait("tensor", self.s_v, st["v_prevbg"])
        half_first = [True, True]
        for i in range(8):
            h = h0 + i
            q = i // 2
            odd = i % 2
            out = self.PSB[64 * odd:64 * odd + 64, q * 128:(q + 1) * 128]
            tp = (0, 64) if odd else None
            sk = bool(odd)
            self.op("tensor", "matmul", out,
                    fr["XDT"][:, h * 64:(h + 1) * 64], gl8[:, i * 128:(i + 1) * 128],
                    start=half_first[odd], stop=False, tile_position=tp,
                    skip_group_check=sk)
            half_first[odd] = False
            self.op("tensor", "matmul", out,
                    prevbg[:, i * 64:(i + 1) * 64], chx8[:, i * 128:(i + 1) * 128],
                    start=False, stop=(i >= 6), tile_position=tp,
                    skip_group_check=sk)
        self.inc("tensor", self.s_t, 1)
        t_y = self.s_t.n
        self.b_gl8[sl].frees.append((self.s_t, t_y))
        self.b_chx8[sl].frees.append((self.s_t, t_y))
        self.b_prevb[sl].frees.append((self.s_t, t_y))
        self.p.acquire("vector", self.b_fr["TSCR"])
        self.wait("vector", self.s_t, t_s8)
        cdb_rep = self.apx(fr["CDB"][:, h0:h0 + 8], [(1, 8), (0, 64)])
        self.op("vector", "tensor_tensor",
                fr["TSCR"][:, 0:512].rearrange("p (h k) -> p h k", h=8),
                self.PREV[:, g * 512:(g + 1) * 512].rearrange("p (h k) -> p h k", h=8),
                cdb_rep, ALU.mult)
        self.op("vector", "tensor_tensor", self.PREV[:, g * 512:(g + 1) * 512],
                fr["TSCR"][:, 0:512], self.PSC[:, 512:1024], ALU.add)
        self.inc("vector", self.s_v, 1)
        self.b_fr["TSCR"].frees.append((self.s_v, self.s_v.n))
        self.b_pscs.frees.append((self.s_v, self.s_v.n))
        self.p.acquire("vector", self.b_fr["DTRM"])
        self.wait("vector", self.s_t, t_y)
        xg_base = XBC[:, (j0 // 4) * 512 + col0:(j0 // 4) * 512 + col0 + 128]
        xg_rep = self.apx(xg_base, [(0, 4), (1, 128)])
        dpc_rep = self.apx(self.DPC[:, j0:j0 + 4], [(1, 4), (0, 128)])
        self.op("vector", "tensor_tensor",
                fr["DTRM"][:, 0:512].rearrange("p (q l) -> p q l", q=4),
                xg_rep, dpc_rep, ALU.mult)
        self.op("vector", "scalar_tensor_tensor", fr["YST"][:, g * 512:(g + 1) * 512],
                self.PSB[:, 0:512], 1.0, fr["DTRM"][:, 0:512], ALU.mult, ALU.add)
        self.inc("vector", self.s_v, 1)
        self.b_fr["DTRM"].frees.append((self.s_v, self.s_v.n))
        self.b_psby.frees.append((self.s_v, self.s_v.n))

    def _fr_psum_handoff(self):
        for src_b, dst in ((self.b_psby, "B"), (self.b_psbx, "B"),
                           (self.b_pscg, "C"), (self.b_pscs, "C")):
            self.b_ps[dst].frees.extend(src_b.frees)
            src_b.frees = []

    def _mamba_back(self):
        A = self.ARN
        bk = dict(
            CARRY=A[:, 0:8192].bitcast(F32),
            DPREV=A[:, 8192:12288],
            YPC=A[:, 12288:20480].bitcast(F32),
            CB=A[:, 20480:24576],
            SZB=A[:, 24576:28672],
            EXPB=[A[:, 50176:52224].bitcast(F32),
                  A[:, 52224:54272].bitcast(F32)],
            CHX=[A[:, 29696:30720], A[:, 61440:62464]],
            TF1=[A[:, 54272 + i * 1024:55296 + i * 1024].bitcast(F32) for i in range(3)],
            TF2=[A[:, 57344 + i * 1024:58368 + i * 1024].bitcast(F32) for i in range(3)],
            SQB=[A[:, 32768:33280], A[:, 60416:60928], A[:, 60928:61440]],
            YN=A[:, 33792:33792 + 4 * 4096],
        )
        self._bk = bk
        self.b_bk = {k: Buf(self.p) for k in ("CARRY", "YN")}
        self.b_ld = [Buf(self.p), Buf(self.p)]
        self.b_dprev = [Buf(self.p) for _ in range(8)]
        self.b_psed = [Buf(self.p), Buf(self.p)]
        self._dpv = {}
        self._edt = {}
        self.b_expb = [Buf(self.p), Buf(self.p)]
        self.b_chxb = [Buf(self.p), Buf(self.p)]
        self.b_tfb = [Buf(self.p) for _ in range(3)]
        self.b_sqb = [Buf(self.p) for _ in range(3)]
        self.b_rowb = [Buf(self.p), Buf(self.p)]
        self.b_psb2 = [Buf(self.p), Buf(self.p)]
        self.b_psc1 = Buf(self.p)
        self.b_psc2 = Buf(self.p)
        for e in ENGINES:
            self.wait(e, self.s_t, self.s_t.n)
            self.wait(e, self.s_v, self.s_v.n)
            self.wait(e, self.s_s, self.s_s.n)
            self.wait(e, self.s_cc, self.s_cc.n)
            self.wait(e, self.s_spc, self.s_spc.n)
            for bb in (self.b_stg[0], self.b_stg[1], self.b_fr["YST"], self.b_iob[0],
                       self.b_iob[1], self.b_carry_out):
                if bb._sem is not None and bb._sem.n:
                    self.wait(e, bb._sem, bb._sem.n)

        cbuf = self.b_bk["CARRY"]
        self.dma("sync", bk["CARRY"][:, :], self.d_carry_rcv[0:128, :], cbuf)
        self.p.wait_buf("vector", cbuf)
        self.op("vector", "tensor_scalar", bk["CARRY"][:, :], bk["CARRY"][:, :],
                self.GAIN[:, 0:1], None, ALU.mult)
        self.inc("vector", self.s_v, 1)
        v_carry = self.s_v.n
        rows = self.ROWS
        CUM = rows[0:1, 0:self.NCH * 64]
        self.p.wait_buf("vector", self.b_clog)
        self.op("vector", "tensor_scalar", CUM[0:1, 0:64], self.CLOG[0:1, 0:64],
                0.0, None, ALU.mult)
        for c in range(1, self.NCH):
            self.op("vector", "tensor_tensor", CUM[0:1, c * 64:(c + 1) * 64],
                    CUM[0:1, (c - 1) * 64:c * 64],
                    self.CLOG[0:1, (c - 1) * 64:c * 64], ALU.subtract)
        self.inc("vector", self.s_v, 1)
        EXPD = rows[0:1, 1024:1024 + self.NCH * 64]
        self.wait("scalar", self.s_v, self.s_v.n)
        self.op("scalar", "activation", EXPD, CUM, AF.Exp, bias=0.0, scale=1.0)
        self.inc("scalar", self.s_s, 1)
        s_expd = self.s_s.n

        for bb in range(self.NMB):
            for c_loc in range(4):
                self._back_chunk(bb, c_loc, v_carry, s_expd)
            self._out_proj_block(bb)

    def _bk_prep_loads(self, ch, h):
        bk = self._bk
        sl = slice(h * 2048, (h + 1) * 2048)
        hb = slice(ch * 4096 + h * 2048, ch * 4096 + (h + 1) * 2048)
        b = self.b_ld[h]
        self.p.acquire("sync", b)
        for nm, srcd in (("YPC", self.d_yp), ("CB", self.d_c), ("SZB", self.d_sz)):
            self.dma("sync", bk[nm][:, sl], srcd[:, hb], b)

    def _bk_prep_ed(self, ch, s_expd):
        slp = ch % 2
        self.p.acquire("tensor", self.b_psed[slp])
        self.wait("tensor", self.s_s, s_expd)
        ps_ed = self.PSD[:, slp * 64:slp * 64 + 64]
        self.op("tensor", "matmul", ps_ed, self.ONR[:, :],
                self._bk_expd_row(ch), start=True, stop=True)
        self.inc("tensor", self.s_t, 1)
        self._edt[ch] = self.s_t.n

    def _bk_prep_dprev(self, ch, g, v_carry):
        bk = self._bk
        slp = ch % 2
        ps_ed = self.PSD[:, slp * 64:slp * 64 + 64]
        b = self.b_dprev[g]
        self.p.acquire("vector", b)
        self.wait("vector", self.s_t, self._edt[ch])
        self.wait("vector", self.s_v, v_carry)
        ed_rep = self.apx(ps_ed[:, g * 8:g * 8 + 8], [(1, 8), (0, 64)])
        self.op("vector", "tensor_tensor",
                bk["DPREV"][:, g * 512:(g + 1) * 512].rearrange("p (h k) -> p h k", h=8),
                bk["CARRY"][:, g * 512:(g + 1) * 512].rearrange("p (h k) -> p h k", h=8),
                ed_rep, ALU.mult)
        self.inc("vector", self.s_v, 1)
        v = self.s_v.n
        self._dpv[(ch, g)] = v
        self.b_psed[slp].frees.append((self.s_v, v))
        self.b_ps["D"].frees.append((self.s_v, v))

    def _back_chunk(self, bb, c_loc, v_carry, s_expd):
        ch = bb * 4 + c_loc
        if ch == 0:
            for h in range(2):
                self._bk_prep_loads(ch, h)
            self._bk_prep_ed(ch, s_expd)
            for g in range(8):
                self._bk_prep_dprev(ch, g, v_carry)
        nxt = ch + 1 if ch + 1 < self.NCH else None
        self._bkst = {}
        for g in range(8):
            self._bk_A(ch, g)
            if g >= 2:
                self._bk_C(ch, g - 2)
            self._bk_B(ch, g)
            if nxt is not None:
                if g == 0:
                    self._bk_prep_ed(nxt, s_expd)
                self._bk_prep_dprev(nxt, g, v_carry)
            if g >= 2:
                self._bk_D(ch, g - 2)
                if nxt is not None and g == 5:
                    self._bk_prep_loads(nxt, 0)
        for g in (6, 7):
            self._bk_C(ch, g)
            self._bk_D(ch, g)
        if nxt is not None:
            self._bk_prep_loads(nxt, 1)

    def _bk_expd_row(self, ch):
        return self.ROWS[0:1, 1024 + ch * 64:1024 + (ch + 1) * 64]

    def _bk_A(self, ch, g):
        bk = self._bk
        h0 = 8 * g
        j0 = 4 * g
        self.p.acquire("tensor", self.b_ps["A"])
        for i in range(8):
            h = h0 + i
            dah = self.DAHL[:, ch * 64 + h:ch * 64 + h + 1]
            dal = self.DAHL[:, (self.NCH + ch) * 64 + h:(self.NCH + ch) * 64 + h + 1]
            self.op("tensor", "matmul", self.PSA[:, i * 128:(i + 1) * 128],
                    dah.broadcast_to([128, 128]), self.ULEB[:, :],
                    start=(i == 0 or i == 4), stop=False)
            self.op("tensor", "matmul", self.PSA[:, i * 128:(i + 1) * 128],
                    dal.broadcast_to([128, 128]), self.ULEB[:, :],
                    start=False, stop=(i == 3 or i == 7))
        self.inc("tensor", self.s_t, 1)
        t_rc = self.s_t.n
        sl2 = g % 2
        be = self.b_expb[sl2]
        expb = bk["EXPB"][sl2]
        self.p.acquire("scalar", be)
        self.wait("scalar", self.s_t, t_rc)
        self.op("scalar", "activation", expb[:, :], self.PSA[:, :], AF.Exp,
                bias=0.0, scale=1.0)
        self.inc("scalar", self.s_s, 1)
        s_eb = self.s_s.n
        self.b_ps["A"].frees.append((self.s_s, s_eb))
        bc = self.b_chxb[sl2]
        chx = bk["CHX"][sl2]
        self.p.acquire("vector", bc)
        self.wait("vector", self.s_s, s_eb)
        self.p.wait_buf("vector", self.b_ld[g // 4])
        c_rep = self.apx(bk["CB"][:, j0 * 128:j0 * 128 + 128],
                         [(128, 4), (0, 2), (1, 128)])
        self.op("vector", "tensor_tensor",
                chx[:, :].rearrange("p (q h l) -> p q h l", q=4, h=2),
                expb[:, :].rearrange("p (q h l) -> p q h l", q=4, h=2),
                c_rep, ALU.mult)
        self.inc("vector", self.s_v, 1)
        be.frees.append((self.s_v, self.s_v.n))
        self.b_ld[g // 4].frees.append((self.s_v, self.s_v.n))
        self._bkst[g] = dict(v_chx=self.s_v.n)

    def _bk_B(self, ch, g):
        bk = self._bk
        st = self._bkst[g]
        h0 = 8 * g
        sl2 = g % 2
        sl3 = g % 3
        chx = bk["CHX"][sl2]
        psb = self.PSB[:, sl2 * 512:(sl2 + 1) * 512]
        self.p.acquire("tensor", self.b_psb2[sl2])
        self.wait("tensor", self.s_v, st["v_chx"])
        self.wait("tensor", self.s_v, self._dpv[(ch, g)])
        half_first = [True, True]
        for i in range(8):
            h = h0 + i
            q = i // 2
            odd = i % 2
            out = psb[64 * odd:64 * odd + 64, q * 128:(q + 1) * 128]
            tp = (0, 64) if odd else None
            self.op("tensor", "matmul", out,
                    bk["DPREV"][:, h * 64:(h + 1) * 64],
                    chx[:, i * 128:(i + 1) * 128],
                    start=half_first[odd], stop=(i >= 6), tile_position=tp,
                    skip_group_check=bool(odd))
            half_first[odd] = False
        self.inc("tensor", self.s_t, 1)
        t_dy = self.s_t.n
        self.b_chxb[sl2].frees.append((self.s_t, t_dy))
        self.b_dprev[g].frees.append((self.s_t, t_dy))
        tf1 = bk["TF1"][sl3]
        tf2 = bk["TF2"][sl3]
        bt = self.b_tfb[sl3]
        self.p.acquire("vector", bt)
        self.wait("vector", self.s_t, t_dy)
        self.p.wait_buf("vector", self.b_ld[g // 4])
        self.op("vector", "tensor_tensor", tf1[:, 0:512],
                bk["YPC"][:, g * 512:(g + 1) * 512], psb, ALU.add)
        self.op("vector", "tensor_tensor", tf2[:, 0:512],
                tf1[:, 0:512], bk["SZB"][:, g * 512:(g + 1) * 512], ALU.mult)
        self.inc("vector", self.s_v, 1)
        v_tf2 = self.s_v.n
        self.b_psb2[sl2].frees.append((self.s_v, v_tf2))
        sq = bk["SQB"][sl3]
        bs = self.b_sqb[sl3]
        self.p.acquire("scalar", bs)
        self.wait("scalar", self.s_v, v_tf2)
        self.op("scalar", "activation", sq[:, :], tf2[:, 0:512],
                AF.Square, bias=0.0, scale=1.0)
        self.inc("scalar", self.s_s, 1)
        st["s_sq"] = self.s_s.n

    def _bk_C(self, ch, g):
        bk = self._bk
        st = self._bkst[g]
        sl3 = g % 3
        self.p.acquire("tensor", self.b_psc1)
        self.wait("tensor", self.s_s, st["s_sq"])
        self.op("tensor", "matmul", self.PSC[0:1, 0:512], self.ONCB[:, :],
                bk["SQB"][sl3][:, :], start=True, stop=True)
        self.inc("tensor", self.s_t, 1)
        t_ms = self.s_t.n
        self.b_sqb[sl3].frees.append((self.s_t, t_ms))
        rstd = self.ROWS[0:1, 2048 + (g % 2) * 512:2560 + (g % 2) * 512]
        self.p.acquire("scalar", self.b_rowb[g % 2])
        self.wait("scalar", self.s_t, t_ms)
        self.op("scalar", "activation", rstd, self.PSC[0:1, 0:512], AF.Ln,
                bias=self.ACONST[0:1, 0:1], scale=self.ACONST[0:1, 2:3])
        self.schain()
        self.op("scalar", "activation", rstd, rstd, AF.Exp, bias=0.0,
                scale=self.ACONST[0:1, 1:2])
        self.inc("scalar", self.s_s, 1)
        st["s_rstd"] = self.s_s.n
        st["rstd"] = rstd
        self.b_psc1.frees.append((self.s_s, self.s_s.n))

    def _bk_D(self, ch, g):
        bk = self._bk
        st = self._bkst[g]
        j0 = 4 * g
        sl3 = g % 3
        self.p.acquire("tensor", self.b_psc2)
        self.wait("tensor", self.s_s, st["s_rstd"])
        self.op("tensor", "matmul", self.PSC[:, 512:1024], self.ONR[:, :], st["rstd"],
                start=True, stop=True)
        self.inc("tensor", self.s_t, 1)
        t_rb = self.s_t.n
        self.b_rowb[g % 2].frees.append((self.s_t, t_rb))
        self.wait("vector", self.s_t, t_rb)
        byn = self.b_bk["YN"]
        if ch % 4 == 0 and g == 0:
            self.p.acquire("vector", byn)
        tf2 = bk["TF2"][sl3]
        yslot = bk["YN"][:, (ch % 4) * 4096 + g * 512:(ch % 4) * 4096 + (g + 1) * 512]
        self.op("vector", "tensor_tensor", yslot,
                tf2[:, 0:512], self.PSC[:, 512:1024], ALU.mult)
        self.inc("vector", self.s_v, 1)
        v_yn = self.s_v.n
        self.b_psc2.frees.append((self.s_v, v_yn))
        self.b_tfb[sl3].frees.append((self.s_v, v_yn))
        self.b_ld[g // 4].frees.append((self.s_v, v_yn))

    def _out_proj_block(self, bb):
        bk = self._bk
        byn = self.b_bk["YN"]
        v_yn = self.s_v.n
        for bwb in self.b_wr:
            self.p.acquire("gpsimd", bwb)
        for m in range(16):
            slot = m % 2
            wslot = m % 3
            wt = self.WR[:, wslot * 4096: wslot * 4096 + 4096]
            bw = self.b_wr[wslot]
            self.p.acquire("gpsimd", bw)
            if self._c_outp:
                self.dma("gpsimd", wt, self.d_wbout[m], bw)
            else:
                self.dma("gpsimd", wt,
                         self.din["w_outT"][m].rearrange("f c d -> f (c d)"), bw)
                self.p.wait_buf("vector", bw)
                nwrep = self.apx(self.NWPC[:, 0:32], [(1, 32), (0, 128)])
                self.op("vector", "tensor_tensor",
                        wt.rearrange("p (j d) -> p j d", j=32),
                        wt.rearrange("p (j d) -> p j d", j=32), nwrep, ALU.mult)
                self.inc("vector", self.s_v, 1)
                self._vsc = self.s_v.n
                self.wait("sync", self.s_v, self._vsc)
                self.dma("sync", self.d_wbout[m], wt, self.b_wst[wslot])
                bw.frees.append(self.b_wst[wslot].ready)
            ps = self.PSD[:, 0:512] if slot == 0 else self.PSD[:, 512:1024]
            self.p.acquire("tensor", self.b_ps["D"])
            self.p.wait_buf("tensor", bw)
            if not self._c_outp:
                self.wait("tensor", self.s_v, self._vsc)
            if m == 0:
                self.wait("tensor", self.s_v, v_yn)
            for j in range(32):
                rhs = self.apx(bk["YN"][:, j * 128:j * 128 + 128], [(4096, 4), (1, 128)])
                self.op("tensor", "matmul", ps, wt[:, j * 128:(j + 1) * 128], rhs,
                        start=(j == 0), stop=(j == 31))
            self.inc("tensor", self.s_t, 1)
            bw.frees.append((self.s_t, self.s_t.n))
            b2 = self.b_iob[slot]
            self.p.acquire("sync", b2)
            xres = self.IOB[:, slot * 1024: slot * 1024 + 512]
            self.dma("sync", xres, self.d_r1[m * 128:(m + 1) * 128,
                                             bb * 512:(bb + 1) * 512], b2)
            self.p.wait_buf("vector", b2)
            self.wait("vector", self.s_t, self.s_t.n)
            self.op("vector", "tensor_tensor", xres, xres, ps, ALU.add)
            self.inc("vector", self.s_v, 1)
            b2.ready = (self.s_v, self.s_v.n)
            self.b_ps["D"].frees.append((self.s_v, self.s_v.n))
            self.p.wait_buf("sync", b2)
            self.dma("sync", self.d_r2[m * 128:(m + 1) * 128, bb * 512:(bb + 1) * 512],
                     xres, b2)
            b2.frees.append((b2.sem, b2.sem.n))
        byn.frees.append((self.s_t, self.s_t.n))
        self._c_outp = True



def kernel(**inputs):
    cores = list(range(8))
    T = 2048
    in_maps = stage_inputs(inputs, cores, T)
    k = Kern(T=T, pair_groups=[[0, 1], [2, 3], [4, 5], [6, 7]])
    res = run_bass_kernel_spmd(k.nc, in_maps, core_ids=cores)
    u = np.asarray(inputs["L0_attn_u"])
    out = np.empty_like(u, dtype=np.float32)
    for i, c in enumerate(cores):
        s, h = c // 2, c % 2
        out[s, h * T:(h + 1) * T, :] = res.results[i]["out_fm"].T
    return out
